# version 01
# baseline (speedup 1.0000x reference)
import contextlib
import numpy as np
import ml_dtypes
import concourse.bass as bass
import concourse.mybir as mybir
from concourse.bass_utils import run_bass_kernel_spmd

F32 = mybir.dt.float32
BF16 = mybir.dt.bfloat16
U8 = mybir.dt.uint8
I32 = mybir.dt.int32
AF = mybir.ActivationFunctionType
ALU = mybir.AluOpType
AX = mybir.AxisListType

B, L, D = 4, 4096, 2048
HID = 5632
NCH = 32
T = 2048
NT = 16
HB = 44
TH = 1024
PI = float(np.pi)
TWO_PI = float(2.0 * np.pi)
EPS_LN = 1e-5
KB = 1024


class Ctx:

    KEYS = ("d_par", "d_u", "d_x0", "d_x1w", "d_x1r", "d_xnw", "d_xnr",
            "d_w", "d_yw", "d_yr", "d_out", "v", "s", "t", "g")

    def __init__(self):
        self.marks = {}
        self.pass_num = 0
        self.cnt = {}
        self.sems = None

    def begin(self, pass_num, sems=None):
        self.pass_num = pass_num
        self.cnt = {k: 0 for k in self.KEYS}
        self.sems = sems

    def inc(self, inst, key, by=1, mark=None):
        if self.pass_num == 2:
            inst.then_inc(self.sems[key], by)
        self.cnt[key] += by
        if mark is not None:
            if self.pass_num == 1:
                assert mark not in self.marks, f"dup mark {mark}"
                self.marks[mark] = (key, self.cnt[key])
            else:
                assert self.marks[mark] == (key, self.cnt[key]), \
                    f"mark drift {mark}"
        return self.cnt[key]

    def wait(self, eng, mark, delta=0):
        if self.pass_num == 2:
            key, val = self.marks[mark]
            if val + delta > 0:
                eng.wait_ge(self.sems[key], val + delta)

    def wait_cnt(self, eng, key, val):
        if self.pass_num == 2 and val > 0:
            eng.wait_ge(self.sems[key], val)

    def selfwait(self, eng, key):
        if self.pass_num == 2 and self.cnt[key] > 0:
            eng.wait_ge(self.sems[key], self.cnt[key])


class _FakeInst:
    def then_inc(self, *a, **k):
        return self


class _FakeEng:
    def __getattr__(self, name):
        return lambda *a, **k: _FakeInst()


def build_program():
    nc = bass.Bass("TRN2", target_bir_lowering=False, debug=False)

    d_in = {}

    def din(name, shape, dt):
        d_in[name] = nc.dram_tensor(name, shape, dt, kind="ExternalInput").ap()

    din("u", [T, D], F32)
    din("uprev", [T, D], F32)
    din("wg", [HB, 128, D], U8)
    din("wu", [HB, 128, D], U8)
    din("wd", [HB, 128, D], U8)
    din("sg", [128, HB], F32)
    din("su", [128, HB], F32)
    din("sd", [1, D], F32)
    din("bre", [NCH, D], F32)
    din("bim", [NCH, D], F32)
    din("cret", [NCH, D], F32)
    din("cimt", [NCH, D], F32)
    din("drow", [1, D], F32)
    din("lunc", [NCH, 1], F32)
    din("lim", [NCH, 1], F32)
    din("logd", [NCH, 1], F32)
    out_hbm = nc.dram_tensor("out", [T, D], F32, kind="ExternalOutput").ap()
    x1_spill = nc.dram_tensor("x1sp", [T, D], F32, kind="Internal").ap()
    xnt_spill = nc.dram_tensor("xntsp", [128, 16 * TH], BF16,
                               kind="Internal").ap()
    y_spill = nc.dram_tensor("ysp", [TH, D], F32, kind="Internal").ap()

    st = contextlib.ExitStack()
    ARENA = 207 * KB
    arena = st.enter_context(nc.sbuf_tensor("arena", [128, ARENA], U8)).ap()

    def vf32(off_kb, n):
        off = int(off_kb * KB)
        assert off % 4 == 0 and off + 4 * n <= ARENA, (off_kb, n)
        return arena[:, off:off + 4 * n].bitcast(F32)

    def vbf(off_kb, n):
        off = int(off_kb * KB)
        assert off % 2 == 0 and off + 2 * n <= ARENA, (off_kb, n)
        return arena[:, off:off + 2 * n].bitcast(BF16)

    def vi32(off_kb, n):
        off = int(off_kb * KB)
        return arena[:, off:off + 4 * n].bitcast(I32)

    t1n_full = vbf(0, 32768)
    wd_bf = vbf(0, 22528)
    wgbuf = [vbf(44, 2048), vbf(48, 2048)]
    wubuf = [vbf(52, 2048), vbf(56, 2048)]
    us_s1 = vf32(60, 1024)
    yrbuf = vf32(0, 2048)
    x1rbuf = vf32(8, 2048)
    outbuf = [vf32(16, 2048), vf32(24, 2048)]

    xnt = vbf(64, 16384)
    drow_f = vf32(70, 2048)[0:1, :]
    sdrow = vf32(78, 2048)[0:1, :]
    drow_bf = vbf(86, 2048)[0:1, :]

    ubuf = [vf32(96, 2048), vf32(104, 2048)]
    t1nT = [vbf(112, 2048), vbf(116, 2048)]
    bu_re_m = vf32(120, 2048)[0:NCH, :]
    bu_im_m = vf32(128, 2048)[0:NCH, :]
    bu_re_p = vf32(136, 2048)[0:NCH, :]
    bu_im_p = vf32(144, 2048)[0:NCH, :]
    cosT = vf32(152, 2048)[0:NCH, :]
    sinT = vf32(160, 2048)[0:NCH, :]
    it_fwd = vf32(168, 2048)[0:NCH, :]
    rp_v = vf32(168, 2048)[0:NCH, :]
    c2re_v = vbf(176, 2048)[0:NCH, :]
    c2im_v = vbf(180, 2048)[0:NCH, :]
    bbar_sA = vf32(96, 2048)[0:NCH, :]
    bbar_sB = vf32(104, 2048)[0:NCH, :]
    bbar_re = vbf(112, 2048)[0:NCH, :]
    bbar_im = vbf(116, 2048)[0:NCH, :]
    ph_p = vf32(120, 2048)[0:NCH, :]
    ph_q = vf32(128, 2048)[0:NCH, :]
    ph_qi = vi32(136, 2048)[0:NCH, :]
    ph_r = vf32(144, 2048)[0:NCH, :]
    tA = vf32(96, 2048)[0:NCH, :]
    tB = vf32(104, 2048)[0:NCH, :]
    tC = vf32(112, 2048)[0:NCH, :]
    bt_re = vf32(136, 2048)[0:NCH, :]
    bt_im = vf32(144, 2048)[0:NCH, :]
    xs_re = vbf(120, 2048)[0:NCH, :]
    xs_im = vbf(124, 2048)[0:NCH, :]
    tmpD = vf32(96, 2048)
    ybuf = vf32(104, 2048)
    x1buf = [vf32(112, 2048), vf32(128, 2048)]
    x0buf = [vf32(136, 2048), vf32(144, 2048)]
    xnbuf = vbf(152, 2048)
    xnt_stage = vbf(156, 2048)
    Hbuf = vbf(96, 45056)

    d_rep_bf = vbf(184, 2048)
    sdrep = vf32(188, 2048)

    id_bf = vbf(196, 128)
    onesrow_bf = vbf(196.25, 128)[0:1, :]
    onesrow_f = vf32(196.5, 128)[0:1, :]
    sgbuf = vf32(197, HB)
    subuf = vf32(197.25, HB)
    statbuf = vf32(197.5, 24)
    mvbuf = vf32(197.75, 2)
    colbuf = vf32(198, 32)
    bret_bf = vbf(198.25, 512)
    bimt_bf = vbf(199.25, 512)
    ystage = [vf32(200.25, 512), vf32(202.25, 512)]
    ln4stats = vf32(204.25, 192)
    h2buf = vbf(205, 1024)

    pA = st.enter_context(nc.psum_tensor("pA", [128, 2048], F32)).ap()
    pB = st.enter_context(nc.psum_tensor("pB", [128, 1024], F32)).ap()
    pC = st.enter_context(nc.psum_tensor("pC", [128, 512], F32)).ap()
    pD = st.enter_context(nc.psum_tensor("pD", [128, 512], F32)).ap()
    pBbf = pB.bitcast(BF16)

    sems = {k: st.enter_context(nc.semaphore(k)) for k in Ctx.KEYS}
    C = Ctx()

    p_lunc = colbuf[0:NCH, 0:1]
    p_lim = colbuf[0:NCH, 1:2]
    p_logd = colbuf[0:NCH, 2:3]
    c_delta = colbuf[0:NCH, 3:4]
    c_lamre = colbuf[0:NCH, 4:5]
    c_lnr = colbuf[0:NCH, 5:6]
    c_theta = colbuf[0:NCH, 6:7]
    c_r = colbuf[0:NCH, 7:8]
    c_are = colbuf[0:NCH, 8:9]
    c_aim = colbuf[0:NCH, 9:10]
    c_t1 = colbuf[0:NCH, 10:11]
    c_t2 = colbuf[0:NCH, 11:12]
    c_t3 = colbuf[0:NCH, 12:13]
    c_t4 = colbuf[0:NCH, 13:14]
    c_dre = colbuf[0:NCH, 14:15]
    c_dim = colbuf[0:NCH, 15:16]
    c_eps = colbuf[:, 16:17]
    c_cre = colbuf[0:NCH, 17:18]
    c_cim = colbuf[0:NCH, 18:19]
    c_sr = colbuf[0:NCH, 19:20]
    c_si = colbuf[0:NCH, 20:21]
    c_negm = colbuf[:, 21:22]
    c_rstd = colbuf[:, 22:23]
    c_sd = colbuf[:, 23:24]
    c_Tlnr = colbuf[0:NCH, 24:25]
    c_i32 = colbuf.bitcast(I32)[0:NCH, 25:26]
    c_sinth = colbuf[0:NCH, 26:27]
    c_sh = colbuf[0:NCH, 27:28]
    c_costh = colbuf[0:NCH, 28:29]
    c_thred = colbuf[0:NCH, 29:30]
    p_bre = bu_re_m
    p_bim = bu_im_m
    p_cret = cosT
    p_cimt = sinT

    def Hcols(hb, tg):
        return Hbuf[:, hb * TH + tg * 512: hb * TH + (tg + 1) * 512]

    def emit_sync(q):
        C.inc(q.dma_start(p_lunc, d_in["lunc"]), "d_par", 16)
        C.inc(q.dma_start(p_lim, d_in["lim"]), "d_par", 16)
        C.inc(q.dma_start(p_logd, d_in["logd"]), "d_par", 16)
        C.inc(q.dma_start(p_bre, d_in["bre"]), "d_par", 16)
        C.inc(q.dma_start(p_bim, d_in["bim"]), "d_par", 16)
        C.inc(q.dma_start(p_cret, d_in["cret"]), "d_par", 16)
        C.inc(q.dma_start(p_cimt, d_in["cimt"]), "d_par", 16)
        C.inc(q.dma_start(drow_f, d_in["drow"]), "d_par", 16)
        C.inc(q.dma_start(sdrow, d_in["sd"]), "d_par", 16)
        C.inc(q.dma_start(sgbuf[:, :], d_in["sg"]), "d_par", 16)
        C.inc(q.dma_start(subuf[:, :], d_in["su"]), "d_par", 16,
              mark="d_params")

        for j in range(2 * NT):
            src = d_in["uprev"] if j < NT else d_in["u"]
            i = j % NT
            if j < 2:
                C.wait(q, "v_bbar")
            else:
                C.wait(q, f"uapply_{j-2}")
            C.inc(q.dma_start(ubuf[j % 2][:, :],
                              src[i * 128:(i + 1) * 128, :]),
                  "d_u", 16, mark=f"du_{j}")

        for i in range(NT):
            if i < 2:
                C.wait(q, "v_scan_done")
            else:
                C.wait(q, f"x1_{i-2}")
            C.inc(q.dma_start(x0buf[i % 2][:, :],
                              d_in["u"][i * 128:(i + 1) * 128, :]),
                  "d_x0", 16, mark=f"dx0_{i}")
            if i >= 1:
                C.wait(q, f"x1_{i-1}")
                C.inc(q.dma_start(x1_spill[(i - 1) * 128:i * 128, :],
                                  x1buf[(i - 1) % 2][:, :]), "d_x1w", 16,
                      mark=f"dx1w_{i-1}")
            if i >= 9:
                C.wait(q, f"xnev_{i-1}")
                C.inc(q.dma_start(
                    xnt_spill.rearrange("p (k ti tt) -> p k ti tt",
                                        k=16, ti=8)[:, :, i - 9, :],
                    xnt_stage.rearrange("p (k tt) -> p k tt", k=16)),
                    "d_xnw", 16, mark=f"dxnw_{i-9}")
        C.wait(q, f"x1_{NT-1}")
        C.inc(q.dma_start(x1_spill[(NT - 1) * 128:NT * 128, :],
                          x1buf[(NT - 1) % 2][:, :]), "d_x1w", 16,
              mark=f"dx1w_{NT-1}")
        C.wait(q, f"xnev_{NT-1}")
        C.inc(q.dma_start(
            xnt_spill.rearrange("p (k ti tt) -> p k ti tt", k=16, ti=8)[
                :, :, 7, :],
            xnt_stage.rearrange("p (k tt) -> p k tt", k=16)),
            "d_xnw", 16, mark="dxnw_7")

        for X in range(2):
            if X == 1:
                C.wait(q, "s1mm_0_43")
                C.inc(q.dma_start(xnt[:, :], xnt_spill[:, :]), "d_xnr", 16,
                      mark="dxnr")
            for dg in range(4):
                for ti in range(8):
                    par = (dg * 8 + ti) % 2
                    C.wait(q, f"ys_{X}_{dg}_{ti}")
                    C.inc(q.dma_start(
                        y_spill[ti * 128:(ti + 1) * 128,
                                dg * 512:(dg + 1) * 512],
                        ystage[par][:, :]), "d_yw", 16,
                        mark=f"dyw_{X}_{dg}_{ti}")
            C.wait(q, f"s2mm_{X}_3_7")
            for ti in range(8):
                gt = X * 8 + ti
                if ti >= 1:
                    C.wait(q, f"ln4apply_{X}_{ti-1}")
                C.inc(q.dma_start(yrbuf[:, :],
                                  y_spill[ti * 128:(ti + 1) * 128, :]),
                      "d_yr", 16, mark=f"dyr_{X}_{ti}")
                if ti >= 1:
                    C.wait(q, f"out_{X}_{ti-1}")
                C.inc(q.dma_start(x1rbuf[:, :],
                                  x1_spill[gt * 128:(gt + 1) * 128, :]),
                      "d_x1r", 16, mark=f"dx1r_{X}_{ti}")
                C.wait(q, f"out_{X}_{ti}")
                C.inc(q.dma_start(out_hbm[gt * 128:(gt + 1) * 128, :],
                                  outbuf[ti % 2][:, :]), "d_out", 16,
                      mark=f"dout_{X}_{ti}")
        C.wait_cnt(q, "d_out", 16 * 16)

    def emit_gpsimd(g):
        C.inc(g.iota(it_fwd[:, :], pattern=[[1, T]], base=1,
                     channel_multiplier=0,
                     allow_small_or_imprecise_dtypes=True), "g", mark="g_itf")
        C.inc(g.memset(id_bf[:, :], 1.0), "g")
        C.selfwait(g, "g")
        C.inc(g.affine_select(id_bf[:, :], id_bf[:, :], pattern=[[-1, 128]],
                              compare_op=ALU.is_equal, fill=0.0, base=0,
                              channel_multiplier=1), "g", mark="g_id")
        C.inc(g.memset(onesrow_bf[:, :], 1.0), "g")
        C.inc(g.memset(onesrow_f[:, :], 1.0), "g")
        C.inc(g.memset(c_eps, EPS_LN), "g", mark="g_setup")

        for X in range(2):
            for hb in range(HB):
                par = hb % 2
                if X == 0 and hb < 2:
                    C.wait(g, "yev_15")
                else:
                    pX, phb = (X, hb - 2) if hb >= 2 else (0, HB - 2 + hb)
                    C.wait(g, f"s1mm_{pX}_{phb}")
                C.inc(g.dma_start(wgbuf[par][:, :], d_in["wg"][hb, :, :]),
                      "d_w", 16)
                C.inc(g.dma_start(wubuf[par][:, :], d_in["wu"][hb, :, :]),
                      "d_w", 16, mark=f"dwu_{X}_{hb}")
                C.wait(g, f"dwu_{X}_{hb}")
                C.inc(g.tensor_scalar(wgbuf[par][:, :], wgbuf[par][:, :],
                                      -128.0, None, ALU.add), "g")
                C.inc(g.tensor_scalar(wubuf[par][:, :], wubuf[par][:, :],
                                      -128.0, None, ALU.add), "g",
                      mark=f"wconv_{X}_{hb}")
            for dg in range(4):
                if dg == 0:
                    C.wait(g, f"h_{X}_{HB-1}")
                    if X == 1:
                        C.wait_cnt(g, "d_out", 16 * 8)
                else:
                    C.wait(g, f"s2mm_{X}_{dg-1}_7")
                C.inc(g.dma_start(
                    wd_bf.rearrange("p (h dd) -> p h dd", h=HB),
                    d_in["wd"][:, :, dg * 512:(dg + 1) * 512].rearrange(
                        "h p dd -> p h dd")), "d_w", 16, mark=f"dwd_{X}_{dg}")
                C.wait(g, f"dwd_{X}_{dg}")
                C.inc(g.tensor_scalar(wd_bf[:, :], wd_bf[:, :], -128.0, None,
                                      ALU.add), "g", mark=f"wdconv_{X}_{dg}")

    def emit_scalar(s):
        C.wait(s, "d_params")
        C.inc(s.activation(c_delta, p_logd, AF.Exp), "s")
        C.inc(s.activation(c_t1, p_lunc, AF.Abs), "s")
        C.selfwait(s, "s")
        C.inc(s.activation(c_t2, c_t1, AF.Exp, scale=-1.0), "s")
        C.selfwait(s, "s")
        C.inc(s.activation(c_t3, c_t2, AF.Ln, bias=1.0), "s", mark="s_p1")
        C.wait(s, "v_lnr")
        C.inc(s.activation(c_r, c_lnr, AF.Exp), "s", mark="s_r")
        C.wait(s, "v_thred")
        C.inc(s.activation(c_sinth, c_thred, AF.Sin), "s")
        C.inc(s.activation(c_sh, c_thred, AF.Sin, scale=0.5), "s",
              mark="s_sincos")
        C.wait(s, "v_pred")
        C.inc(s.activation(sinT[:, :], ph_p[:, :], AF.Sin), "s")
        C.inc(s.activation(ph_q[:, :], ph_p[:, :], AF.Abs), "s",
              mark="s_abs")
        C.wait(s, "v_carg")
        C.inc(s.activation(cosT[:, :], ph_r[:, :], AF.Sin), "s",
              mark="s_tables")
        C.inc(s.activation(drow_bf, drow_f, AF.Copy), "s", mark="s_dbf")

        for j in range(2 * NT):
            C.wait(s, f"aggr_{j}")
            C.inc(s.activation(c_sd, mvbuf[:, 1:2], AF.Sqrt, bias=c_eps),
                  "s", mark=f"sqrt_{j}")
        C.wait(s, "v_rpe")
        C.inc(s.activation(rp_v[:, :], tC[:, :], AF.Exp, scale=-1.0,
                           bias=c_Tlnr), "s", mark="s_rpow")
        for i in range(NT):
            C.wait(s, f"yaggr_{i}")
            C.inc(s.activation(c_sd, mvbuf[:, 1:2], AF.Sqrt, bias=c_eps),
                  "s", mark=f"ysqrt_{i}")
            C.wait(s, f"x1aggr_{i}")
            C.inc(s.activation(c_sd, mvbuf[:, 1:2], AF.Sqrt, bias=c_eps),
                  "s", mark=f"x1sqrt_{i}")
        for X in range(2):
            for hb in range(HB):
                C.wait(s, f"gmm_{X}_{hb}")
                if hb >= 1 or X == 1:
                    pX, phb = (X, hb - 1) if hb >= 1 else (0, HB - 1)
                    C.wait(s, f"h_{pX}_{phb}")
                for tg in range(2):
                    C.inc(s.activation(h2buf[:, tg * 512:(tg + 1) * 512],
                                       pB[:, tg * 512:(tg + 1) * 512],
                                       AF.Silu, scale=sgbuf[:, hb:hb + 1]),
                          "s", mark=(f"gs_{X}_{hb}" if tg == 1 else None))
            for ti in range(8):
                C.wait(s, f"l4aggr_{X}_{ti}")
                C.inc(s.activation(c_sd, mvbuf[:, 1:2], AF.Sqrt, bias=c_eps),
                      "s", mark=f"l4sqrt_{X}_{ti}")

    def emit_tensor(t):
        C.wait(t, "g_id")
        C.wait(t, "v_bbar")
        for kt in range(16):
            C.inc(t.transpose(pBbf[:, kt * 32:(kt + 1) * 32],
                              bbar_re[:, kt * 128:(kt + 1) * 128],
                              id_bf[0:NCH, 0:NCH]), "t")
        for kt in range(16):
            C.inc(t.transpose(pBbf[:, 512 + kt * 32:512 + (kt + 1) * 32],
                              bbar_im[:, kt * 128:(kt + 1) * 128],
                              id_bf[0:NCH, 0:NCH]), "t",
                  mark=("t_btr" if kt == 15 else None))
        C.wait(t, "s_dbf")
        for dg in range(4):
            C.inc(t.matmul(pA[:, dg * 512:(dg + 1) * 512], onesrow_bf,
                           drow_bf[:, dg * 512:(dg + 1) * 512],
                           start=True, stop=True), "t",
                  mark=("t_drep" if dg == 3 else None))
        C.wait(t, "v_drep")
        for dg in range(4):
            C.inc(t.matmul(pA[:, dg * 512:(dg + 1) * 512], onesrow_f,
                           sdrow[:, dg * 512:(dg + 1) * 512],
                           start=True, stop=True), "t",
                  mark=("t_sdrep" if dg == 3 else None))

        for j in range(2 * NT):
            C.wait(t, f"uapply_{j}")
            if j >= 1:
                C.wait(t, f"t1nT_{j-1}")
            i = j % NT
            src = t1n_full[:, i * 2048:(i + 1) * 2048]
            for kt in range(16):
                C.inc(t.transpose(pBbf[:, kt * 128:(kt + 1) * 128],
                                  src[:, kt * 128:(kt + 1) * 128],
                                  id_bf[:, :]), "t",
                      mark=(f"tr_{j}" if kt == 15 else None))
            C.wait(t, f"t1nT_{j}")
            if j >= 1:
                C.wait(t, f"buev_{j-1}")
            for kt in range(16):
                C.inc(t.matmul(pC[0:NCH, 0:128],
                               bret_bf[:, kt * 32:(kt + 1) * 32],
                               t1nT[j % 2][:, kt * 128:(kt + 1) * 128],
                               start=(kt == 0), stop=(kt == 15)), "t")
            for kt in range(16):
                C.inc(t.matmul(pD[0:NCH, 0:128],
                               bimt_bf[:, kt * 32:(kt + 1) * 32],
                               t1nT[j % 2][:, kt * 128:(kt + 1) * 128],
                               start=(kt == 0), stop=(kt == 15)), "t",
                      mark=(f"bu_{j}" if kt == 15 else None))

        C.wait(t, "v_scan_done")
        C.wait(t, "v_sdrep")
        for i in range(NT):
            if i >= 1:
                C.wait(t, f"yev_{i-1}")
            for dg in range(4):
                C.inc(t.matmul(pA[:, dg * 512:(dg + 1) * 512],
                               xs_re[:, i * 128:(i + 1) * 128],
                               c2re_v[:, dg * 512:(dg + 1) * 512],
                               start=True, stop=False), "t")
                C.inc(t.matmul(pA[:, dg * 512:(dg + 1) * 512],
                               xs_im[:, i * 128:(i + 1) * 128],
                               c2im_v[:, dg * 512:(dg + 1) * 512],
                               start=False, stop=True), "t",
                      mark=(f"ymm_{i}" if dg == 3 else None))
            C.wait(t, f"xnapply_{i}")
            if i == 0:
                C.wait(t, f"buev_{2*NT-1}")
            else:
                C.wait(t, f"xnev_{i-1}")
            for kt in range(16):
                C.inc(t.transpose(pBbf[:, kt * 128:(kt + 1) * 128],
                                  xnbuf[:, kt * 128:(kt + 1) * 128],
                                  id_bf[:, :]), "t",
                      mark=(f"xntr_{i}" if kt == 15 else None))

        for X in range(2):
            for hb in range(HB):
                par = hb % 2
                C.wait(t, f"wconv_{X}_{hb}")
                if X == 0 and hb == 0:
                    C.wait(t, "xnev_7")
                if X == 1 and hb == 0:
                    C.wait(t, "dxnr")
                if hb >= 1 or X == 1:
                    pX, phb = (X, hb - 1) if hb >= 1 else (0, HB - 1)
                    C.wait(t, f"gs_{pX}_{phb}")
                    C.wait(t, f"uev_{pX}_{phb}")
                for tg in range(2):
                    for kt in range(16):
                        C.inc(t.matmul(
                            pB[:, tg * 512:(tg + 1) * 512],
                            wgbuf[par][:, kt * 128:(kt + 1) * 128],
                            xnt[:, kt * TH + tg * 512:
                                kt * TH + (tg + 1) * 512],
                            start=(kt == 0), stop=(kt == 15)), "t",
                            mark=(f"gmm_{X}_{hb}"
                                  if tg == 1 and kt == 15 else None))
                for tg in range(2):
                    ps = pC if tg == 0 else pD
                    for kt in range(16):
                        C.inc(t.matmul(
                            ps[:, :],
                            wubuf[par][:, kt * 128:(kt + 1) * 128],
                            xnt[:, kt * TH + tg * 512:
                                kt * TH + (tg + 1) * 512],
                            start=(kt == 0), stop=(kt == 15)), "t",
                            mark=(f"s1mm_{X}_{hb}"
                                  if tg == 1 and kt == 15 else None))
            for dg in range(4):
                C.wait(t, f"wdconv_{X}_{dg}")
                for ti in range(8):
                    ps = pC if ti % 2 == 0 else pD
                    k = dg * 8 + ti
                    if k >= 2:
                        pk = k - 2
                        C.wait(t, f"ys_{X}_{pk // 8}_{pk % 8}")
                    else:
                        C.wait(t, f"uev_{X}_{HB-1}")
                    for hb in range(HB):
                        C.inc(t.matmul(
                            ps[:, :],
                            Hbuf[:, hb * TH + ti * 128:
                                 hb * TH + (ti + 1) * 128],
                            wd_bf[:, hb * 512:(hb + 1) * 512],
                            start=(hb == 0), stop=(hb == HB - 1)), "t",
                            mark=(f"s2mm_{X}_{dg}_{ti}"
                                  if hb == HB - 1 else None))

    def emit_vector(v):
        C.wait(v, "d_params")
        C.wait(v, "g_setup")
        C.inc(v.tensor_scalar(c2re_v[:, :], p_cret, 2.0, None, ALU.mult),
              "v")
        C.inc(v.tensor_scalar(c2im_v[:, :], p_cimt, -2.0, None, ALU.mult),
              "v", mark="v_c2")
        C.wait(v, "s_p1")
        C.inc(v.tensor_scalar(c_t1, p_lunc, 0.0, None, ALU.max), "v")
        C.selfwait(v, "v")
        C.inc(v.tensor_tensor(c_t1, c_t1, c_t3, ALU.add), "v")
        C.selfwait(v, "v")
        C.inc(v.tensor_scalar(c_lamre, c_t1, 0.0101, -1.0, ALU.add,
                              ALU.mult), "v")
        C.selfwait(v, "v")
        C.inc(v.tensor_tensor(c_lnr, c_lamre, c_delta, ALU.mult), "v",
              mark="v_lnr")
        C.selfwait(v, "v")
        C.inc(v.tensor_scalar(c_Tlnr, c_lnr, float(T), None, ALU.mult), "v")
        C.inc(v.tensor_tensor(c_theta, p_lim, c_delta, ALU.mult), "v")
        C.selfwait(v, "v")
        C.inc(v.tensor_scalar(c_t2, c_theta, 1.0 / TWO_PI, None, ALU.mult),
              "v")
        C.selfwait(v, "v")
        C.inc(v.tensor_copy(c_i32, c_t2), "v")
        C.selfwait(v, "v")
        C.inc(v.tensor_copy(c_t2, c_i32), "v")
        C.selfwait(v, "v")
        C.inc(v.scalar_tensor_tensor(c_thred, c_t2, -TWO_PI, c_theta,
                                     ALU.mult, ALU.add), "v", mark="v_thred")
        C.wait(v, "s_sincos")
        C.inc(v.tensor_tensor(c_t2, c_sh, c_sh, ALU.mult), "v")
        C.selfwait(v, "v")
        C.inc(v.tensor_scalar(c_costh, c_t2, -2.0, 1.0, ALU.mult, ALU.add),
              "v")
        C.wait(v, "s_r")
        C.selfwait(v, "v")
        C.inc(v.tensor_tensor(c_are, c_r, c_costh, ALU.mult), "v")
        C.inc(v.tensor_tensor(c_aim, c_r, c_sinth, ALU.mult), "v")
        C.selfwait(v, "v")
        C.inc(v.tensor_scalar(c_t1, c_are, -1.0, None, ALU.add), "v")
        C.inc(v.tensor_tensor(c_t2, c_lamre, c_lamre, ALU.mult), "v")
        C.inc(v.tensor_tensor(c_t3, p_lim, p_lim, ALU.mult), "v")
        C.selfwait(v, "v")
        C.inc(v.tensor_tensor(c_t2, c_t2, c_t3, ALU.add), "v")
        C.selfwait(v, "v")
        C.inc(v.reciprocal(c_t2, c_t2), "v")
        C.selfwait(v, "v")
        C.inc(v.tensor_tensor(c_t3, c_t1, c_lamre, ALU.mult), "v")
        C.inc(v.tensor_tensor(c_t4, c_aim, p_lim, ALU.mult), "v")
        C.selfwait(v, "v")
        C.inc(v.tensor_tensor(c_t3, c_t3, c_t4, ALU.add), "v")
        C.selfwait(v, "v")
        C.inc(v.tensor_tensor(c_dre, c_t3, c_t2, ALU.mult), "v")
        C.inc(v.tensor_tensor(c_t3, c_aim, c_lamre, ALU.mult), "v")
        C.inc(v.tensor_tensor(c_t4, c_t1, p_lim, ALU.mult), "v")
        C.selfwait(v, "v")
        C.inc(v.tensor_tensor(c_t3, c_t3, c_t4, ALU.subtract), "v")
        C.selfwait(v, "v")
        C.inc(v.tensor_tensor(c_dim, c_t3, c_t2, ALU.mult), "v")
        C.selfwait(v, "v")
        C.inc(v.tensor_scalar(bbar_sA[:, :], p_bre, c_dre, None, ALU.mult),
              "v")
        C.inc(v.tensor_scalar(bbar_sB[:, :], p_bim, c_dim, None, ALU.mult),
              "v")
        C.selfwait(v, "v")
        C.inc(v.tensor_tensor(bbar_re[:, :], bbar_sA[:, :], bbar_sB[:, :],
                              ALU.subtract), "v")
        C.inc(v.tensor_scalar(bbar_sA[:, :], p_bim, c_dre, None, ALU.mult),
              "v")
        C.inc(v.tensor_scalar(bbar_sB[:, :], p_bre, c_dim, None, ALU.mult),
              "v")
        C.selfwait(v, "v")
        C.inc(v.tensor_tensor(bbar_im[:, :], bbar_sA[:, :], bbar_sB[:, :],
                              ALU.add), "v", mark="v_bbar")
        C.inc(v.tensor_scalar(ph_p[:, :], it_fwd[:, :], c_theta, None,
                              ALU.mult), "v")
        C.selfwait(v, "v")
        C.inc(v.tensor_scalar(ph_q[:, :], ph_p[:, :], 1.0 / TWO_PI, None,
                              ALU.mult), "v")
        C.selfwait(v, "v")
        C.inc(v.tensor_copy(ph_qi, ph_q[:, :]), "v")
        C.selfwait(v, "v")
        C.inc(v.tensor_copy(ph_r[:, :], ph_qi), "v")
        C.selfwait(v, "v")
        C.inc(v.scalar_tensor_tensor(ph_p[:, :], ph_r[:, :], -TWO_PI,
                                     ph_p[:, :], ALU.mult, ALU.add), "v",
              mark="v_pred")
        C.wait(v, "s_abs")
        C.inc(v.tensor_scalar(ph_r[:, :], ph_q[:, :], -1.0, PI / 2.0,
                              ALU.mult, ALU.add), "v", mark="v_carg")
        C.wait(v, "t_btr")
        C.inc(v.tensor_copy(bret_bf[:, :], pBbf[:, 0:512]), "v")
        C.inc(v.tensor_copy(bimt_bf[:, :], pBbf[:, 512:1024]), "v",
              mark="v_btrev")
        C.wait(v, "t_drep")
        C.inc(v.tensor_copy(d_rep_bf[:, :], pA[:, :]), "v", mark="v_drep")
        C.wait(v, "t_sdrep")
        C.inc(v.tensor_copy(sdrep[:, :], pA[:, :]), "v", mark="v_sdrep")

        for j in range(2 * NT):
            i = j % NT
            C.wait(v, f"du_{j}")
            for gq in range(4):
                C.inc(v.bn_stats(statbuf[:, 6 * gq:6 * (gq + 1)],
                                 ubuf[j % 2][:, 512 * gq:512 * (gq + 1)]),
                      "v")
            C.selfwait(v, "v")
            C.inc(v.bn_aggr(mvbuf[:, :], statbuf[:, :]), "v",
                  mark=f"aggr_{j}")
            C.wait(v, f"sqrt_{j}")
            C.inc(v.reciprocal(c_rstd, c_sd), "v")
            C.inc(v.tensor_scalar(c_negm, mvbuf[:, 0:1], -1.0, None,
                                  ALU.mult), "v")
            C.selfwait(v, "v")
            C.inc(v.tensor_scalar(t1n_full[:, i * 2048:(i + 1) * 2048],
                                  ubuf[j % 2][:, :], c_negm, c_rstd,
                                  ALU.add, ALU.mult), "v",
                  mark=f"uapply_{j}")
            C.wait(v, f"tr_{j}")
            for gq in range(4):
                C.inc(v.tensor_copy(t1nT[j % 2][:, 512 * gq:512 * (gq + 1)],
                                    pBbf[:, 512 * gq:512 * (gq + 1)]), "v",
                      mark=(f"t1nT_{j}" if gq == 3 else None))
            C.wait(v, f"bu_{j}")
            bre_dst = (bu_re_p if j < NT else bu_re_m)[
                :, i * 128:(i + 1) * 128]
            bim_dst = (bu_im_p if j < NT else bu_im_m)[
                :, i * 128:(i + 1) * 128]
            C.inc(v.tensor_copy(bre_dst, pC[0:NCH, 0:128]), "v")
            C.inc(v.tensor_copy(bim_dst, pD[0:NCH, 0:128]), "v",
                  mark=f"buev_{j}")

        C.inc(v.tensor_scalar(tC[:, :], it_fwd[:, :], c_lnr, None, ALU.mult),
              "v", mark="v_rpe")
        C.wait(v, "s_rpow")
        C.inc(v.tensor_tensor(tA[:, :], cosT[:, :], rp_v[:, :], ALU.mult),
              "v")
        C.inc(v.tensor_tensor(tB[:, :], sinT[:, :], rp_v[:, :], ALU.mult),
              "v")
        C.selfwait(v, "v")
        C.inc(v.tensor_tensor(tC[:, :], tA[:, :], bu_re_p[:, :], ALU.mult),
              "v")
        C.selfwait(v, "v")
        C.inc(v.reduce_sum(c_t1, tC[:, :], axis=AX.X), "v")
        C.inc(v.tensor_tensor(tC[:, :], tB[:, :], bu_im_p[:, :], ALU.mult),
              "v")
        C.selfwait(v, "v")
        C.inc(v.reduce_sum(c_t2, tC[:, :], axis=AX.X), "v")
        C.inc(v.tensor_tensor(tC[:, :], tA[:, :], bu_im_p[:, :], ALU.mult),
              "v")
        C.selfwait(v, "v")
        C.inc(v.reduce_sum(c_t3, tC[:, :], axis=AX.X), "v")
        C.inc(v.tensor_tensor(tC[:, :], tB[:, :], bu_re_p[:, :], ALU.mult),
              "v")
        C.selfwait(v, "v")
        C.inc(v.reduce_sum(c_t4, tC[:, :], axis=AX.X), "v")
        C.selfwait(v, "v")
        C.inc(v.tensor_tensor(c_sr, c_t1, c_t2, ALU.add), "v")
        C.inc(v.tensor_tensor(c_si, c_t3, c_t4, ALU.subtract), "v")
        C.selfwait(v, "v")
        C.inc(v.tensor_tensor(c_t1, cosT[:, T - 1:T], c_sr, ALU.mult), "v")
        C.inc(v.tensor_tensor(c_t2, sinT[:, T - 1:T], c_si, ALU.mult), "v")
        C.inc(v.tensor_tensor(c_t3, sinT[:, T - 1:T], c_sr, ALU.mult), "v")
        C.inc(v.tensor_tensor(c_t4, cosT[:, T - 1:T], c_si, ALU.mult), "v")
        C.selfwait(v, "v")
        C.inc(v.tensor_tensor(c_cre, c_t1, c_t2, ALU.subtract), "v")
        C.inc(v.tensor_tensor(c_cim, c_t3, c_t4, ALU.add), "v")
        C.selfwait(v, "v")
        C.inc(v.tensor_tensor(tA[:, :], cosT[:, :], bu_re_m[:, :], ALU.mult),
              "v")
        C.inc(v.tensor_tensor(tB[:, :], sinT[:, :], bu_im_m[:, :], ALU.mult),
              "v")
        C.selfwait(v, "v")
        C.inc(v.tensor_tensor(bt_re[:, :], tA[:, :], tB[:, :], ALU.add), "v")
        C.inc(v.tensor_tensor(tA[:, :], cosT[:, :], bu_im_m[:, :], ALU.mult),
              "v")
        C.inc(v.tensor_tensor(tB[:, :], sinT[:, :], bu_re_m[:, :], ALU.mult),
              "v")
        C.selfwait(v, "v")
        C.inc(v.tensor_tensor(bt_im[:, :], tA[:, :], tB[:, :], ALU.subtract),
              "v")
        C.inc(v.memset(tC[:, :], 1.0), "v")
        C.selfwait(v, "v")
        C.inc(v.tensor_scalar(tC[:, :], tC[:, :], c_r, None, ALU.mult), "v")
        C.selfwait(v, "v")
        C.inc(v.tensor_tensor_scan(tA[:, :], tC[:, :], bt_re[:, :], c_cre,
                                   ALU.mult, ALU.add), "v")
        C.inc(v.tensor_tensor_scan(tB[:, :], tC[:, :], bt_im[:, :], c_cim,
                                   ALU.mult, ALU.add), "v")
        C.selfwait(v, "v")
        C.inc(v.tensor_tensor(tC[:, :], cosT[:, :], tA[:, :], ALU.mult), "v")
        C.inc(v.tensor_tensor(bt_re[:, :], sinT[:, :], tB[:, :], ALU.mult),
              "v")
        C.selfwait(v, "v")
        C.inc(v.tensor_tensor(xs_re[:, :], tC[:, :], bt_re[:, :],
                              ALU.subtract), "v")
        C.inc(v.tensor_tensor(tC[:, :], cosT[:, :], tB[:, :], ALU.mult), "v")
        C.inc(v.tensor_tensor(bt_im[:, :], sinT[:, :], tA[:, :], ALU.mult),
              "v")
        C.selfwait(v, "v")
        C.inc(v.tensor_tensor(xs_im[:, :], tC[:, :], bt_im[:, :], ALU.add),
              "v", mark="v_scan_done")

        for i in range(NT):
            C.inc(v.tensor_tensor(tmpD[:, :], d_rep_bf[:, :],
                                  t1n_full[:, i * 2048:(i + 1) * 2048],
                                  ALU.mult), "v")
            C.wait(v, f"ymm_{i}")
            C.selfwait(v, "v")
            for gq in range(4):
                C.inc(v.tensor_tensor(ybuf[:, 512 * gq:512 * (gq + 1)],
                                      pA[:, 512 * gq:512 * (gq + 1)],
                                      tmpD[:, 512 * gq:512 * (gq + 1)],
                                      ALU.add), "v",
                      mark=(f"yev_{i}" if gq == 3 else None))
            C.selfwait(v, "v")
            for gq in range(4):
                C.inc(v.bn_stats(statbuf[:, 6 * gq:6 * (gq + 1)],
                                 ybuf[:, 512 * gq:512 * (gq + 1)]), "v")
            C.selfwait(v, "v")
            C.inc(v.bn_aggr(mvbuf[:, :], statbuf[:, :]), "v",
                  mark=f"yaggr_{i}")
            C.wait(v, f"ysqrt_{i}")
            C.inc(v.reciprocal(c_rstd, c_sd), "v")
            C.inc(v.tensor_scalar(c_negm, mvbuf[:, 0:1], -1.0, None,
                                  ALU.mult), "v")
            C.selfwait(v, "v")
            C.inc(v.tensor_scalar(tmpD[:, :], ybuf[:, :], c_negm, c_rstd,
                                  ALU.add, ALU.mult), "v")
            C.wait(v, f"dx0_{i}")
            if i >= 2:
                C.wait(v, f"dx1w_{i-2}")
            C.selfwait(v, "v")
            C.inc(v.tensor_tensor(x1buf[i % 2][:, :], tmpD[:, :],
                                  x0buf[i % 2][:, :], ALU.add), "v",
                  mark=f"x1_{i}")
            C.selfwait(v, "v")
            for gq in range(4):
                C.inc(v.bn_stats(statbuf[:, 6 * gq:6 * (gq + 1)],
                                 x1buf[i % 2][:, 512 * gq:512 * (gq + 1)]),
                      "v")
            C.selfwait(v, "v")
            C.inc(v.bn_aggr(mvbuf[:, :], statbuf[:, :]), "v",
                  mark=f"x1aggr_{i}")
            C.wait(v, f"x1sqrt_{i}")
            C.inc(v.reciprocal(c_rstd, c_sd), "v")
            C.inc(v.tensor_scalar(c_negm, mvbuf[:, 0:1], -1.0, None,
                                  ALU.mult), "v")
            C.selfwait(v, "v")
            C.inc(v.tensor_scalar(xnbuf[:, :], x1buf[i % 2][:, :], c_negm,
                                  c_rstd, ALU.add, ALU.mult), "v",
                  mark=f"xnapply_{i}")
            C.wait(v, f"xntr_{i}")
            if i < 8:
                for gq in range(4):
                    dst = xnt.rearrange("p (k tt) -> p k tt", k=16)[
                        :, 4 * gq:4 * (gq + 1), i * 128:(i + 1) * 128]
                    C.inc(v.tensor_copy(
                        dst,
                        pBbf[:, 512 * gq:512 * (gq + 1)].rearrange(
                            "p (k tt) -> p k tt", k=4)), "v",
                        mark=(f"xnev_{i}" if gq == 3 else None))
            else:
                if i >= 10:
                    C.wait(v, f"dxnw_{i-10}")
                for gq in range(4):
                    C.inc(v.tensor_copy(
                        xnt_stage[:, 512 * gq:512 * (gq + 1)],
                        pBbf[:, 512 * gq:512 * (gq + 1)]), "v",
                        mark=(f"xnev_{i}" if gq == 3 else None))

        for X in range(2):
            for hb in range(HB):
                if X == 0 and hb == 0:
                    C.wait(v, "dxnw_7")
                    C.wait(v, f"dx1w_{NT-1}")
                C.wait(v, f"s1mm_{X}_{hb}")
                for tg in range(2):
                    ps = pC if tg == 0 else pD
                    C.inc(v.tensor_scalar(
                        us_s1[:, tg * 512:(tg + 1) * 512], ps[:, :],
                        subuf[:, hb:hb + 1], None, ALU.mult), "v",
                        mark=(f"uev_{X}_{hb}" if tg == 1 else None))
                C.wait(v, f"gs_{X}_{hb}")
                C.selfwait(v, "v")
                for tg in range(2):
                    C.inc(v.tensor_tensor(
                        Hcols(hb, tg), h2buf[:, tg * 512:(tg + 1) * 512],
                        us_s1[:, tg * 512:(tg + 1) * 512], ALU.mult), "v",
                        mark=(f"h_{X}_{hb}" if tg == 1 else None))
            for dg in range(4):
                for ti in range(8):
                    par = (dg * 8 + ti) % 2
                    C.wait(v, f"s2mm_{X}_{dg}_{ti}")
                    k = dg * 8 + ti
                    if k >= 2:
                        pk = k - 2
                        C.wait(v, f"dyw_{X}_{pk // 8}_{pk % 8}")
                    elif X == 1:
                        C.wait(v, f"dyw_0_3_{6 + ti}")
                    ps = pC if ti % 2 == 0 else pD
                    C.inc(v.tensor_tensor(ystage[par][:, :], ps[:, :],
                                          sdrep[:, dg * 512:(dg + 1) * 512],
                                          ALU.mult), "v")
                    C.selfwait(v, "v")
                    C.inc(v.bn_stats(
                        ln4stats[:, ti * 24 + dg * 6:ti * 24 + dg * 6 + 6],
                        ystage[par][:, :]), "v", mark=f"ys_{X}_{dg}_{ti}")
            for ti in range(8):
                C.wait(v, f"dyr_{X}_{ti}")
                C.inc(v.bn_aggr(mvbuf[:, :],
                                ln4stats[:, ti * 24:(ti + 1) * 24]), "v",
                      mark=f"l4aggr_{X}_{ti}")
                C.wait(v, f"l4sqrt_{X}_{ti}")
                C.inc(v.reciprocal(c_rstd, c_sd), "v")
                C.inc(v.tensor_scalar(c_negm, mvbuf[:, 0:1], -1.0, None,
                                      ALU.mult), "v")
                C.selfwait(v, "v")
                C.inc(v.tensor_scalar(yrbuf[:, :], yrbuf[:, :], c_negm,
                                      c_rstd, ALU.add, ALU.mult), "v",
                      mark=f"ln4apply_{X}_{ti}")
                C.wait(v, f"dx1r_{X}_{ti}")
                gk = X * 8 + ti
                if gk >= 2:
                    pgk = gk - 2
                    C.wait(v, f"dout_{pgk // 8}_{pgk % 8}")
                C.selfwait(v, "v")
                C.inc(v.tensor_tensor(outbuf[ti % 2][:, :], yrbuf[:, :],
                                      x1rbuf[:, :], ALU.add), "v",
                      mark=f"out_{X}_{ti}")

    def run_pass(pn):
        C.begin(pn, sems if pn == 2 else None)
        if pn == 1:
            emit_sync(_FakeEng())
            emit_gpsimd(_FakeEng())
            emit_scalar(_FakeEng())
            emit_vector(_FakeEng())
            emit_tensor(_FakeEng())
            return
        with nc.Block() as block:
            @block.sync
            def _(q):
                emit_sync(q)

            @block.gpsimd
            def _(g):
                emit_gpsimd(g)

            @block.scalar
            def _(s):
                emit_scalar(s)

            @block.vector
            def _(v):
                emit_vector(v)

            @block.tensor
            def _(t):
                emit_tensor(t)

    run_pass(1)
    cnt1 = dict(C.cnt)
    run_pass(2)
    assert C.cnt == cnt1, (C.cnt, cnt1)
    st.close()
    return nc


def _u8view(wq_int32):
    a = np.ascontiguousarray(np.asarray(wq_int32))
    assert a.dtype == np.int32
    return np.ascontiguousarray(a.view(np.uint8)[..., ::4])


def _stage_w_hd(wq):
    a = _u8view(wq).reshape(HB, 128, 16, 128)
    a = np.ascontiguousarray(a.transpose(0, 3, 2, 1))
    return a.reshape(HB, 128, 2048)


def _stage_w_dh(wq):
    a = np.ascontiguousarray(_u8view(wq).T)
    return a.reshape(HB, 128, 2048)


def kernel(L0_attn_u, L0_attn_Lambda_unconstrained_re, L0_attn_Lambda_im,
           L0_attn_B_re, L0_attn_B_im, L0_attn_C_re, L0_attn_C_im,
           L0_attn_D, L0_attn_log_Delta, L0_ffn_wq_gate, L0_ffn_s_gate,
           L0_ffn_wq_up, L0_ffn_s_up, L0_ffn_wq_down, L0_ffn_s_down):
    u = np.ascontiguousarray(np.asarray(L0_attn_u, dtype=np.float32))
    wg = _stage_w_hd(L0_ffn_wq_gate)
    wu = _stage_w_hd(L0_ffn_wq_up)
    wd = _stage_w_dh(L0_ffn_wq_down)
    sg = np.ascontiguousarray(
        np.asarray(L0_ffn_s_gate, np.float32).reshape(HB, 128).T)
    su = np.ascontiguousarray(
        np.asarray(L0_ffn_s_up, np.float32).reshape(HB, 128).T)
    sd = np.ascontiguousarray(np.asarray(L0_ffn_s_down,
                                         np.float32).reshape(1, D))
    bre = np.ascontiguousarray(np.asarray(L0_attn_B_re, np.float32))
    bim = np.ascontiguousarray(np.asarray(L0_attn_B_im, np.float32))
    cret = np.ascontiguousarray(np.asarray(L0_attn_C_re, np.float32).T)
    cimt = np.ascontiguousarray(np.asarray(L0_attn_C_im, np.float32).T)
    drow = np.ascontiguousarray(np.asarray(L0_attn_D,
                                           np.float32).reshape(1, D))
    lunc = np.ascontiguousarray(
        np.asarray(L0_attn_Lambda_unconstrained_re,
                   np.float32).reshape(NCH, 1))
    lim = np.ascontiguousarray(np.asarray(L0_attn_Lambda_im,
                                          np.float32).reshape(NCH, 1))
    logd = np.ascontiguousarray(
        np.asarray(L0_attn_log_Delta, np.float32)[:NCH].reshape(NCH, 1))
    zeros_u = np.zeros((T, D), np.float32)

    nc = build_program()
    in_maps = []
    for c in range(8):
        b, h = c // 2, c % 2
        mine = np.ascontiguousarray(u[b, h * T:(h + 1) * T, :])
        prev = np.ascontiguousarray(u[b, 0:T, :]) if h == 1 else zeros_u
        in_maps.append(dict(
            u=mine, uprev=prev, wg=wg, wu=wu, wd=wd, sg=sg, su=su, sd=sd,
            bre=bre, bim=bim, cret=cret, cimt=cimt, drow=drow, lunc=lunc,
            lim=lim, logd=logd))
    res = run_bass_kernel_spmd(nc, in_maps, core_ids=list(range(8)))
    out = np.empty((B, L, D), np.float32)
    for c in range(8):
        b, h = c // 2, c % 2
        out[b, h * T:(h + 1) * T, :] = np.asarray(res.results[c]["out"])
    return out


# version 03
# speedup vs baseline: 2.9737x; 2.9737x over previous
import contextlib
import numpy as np
import ml_dtypes
import concourse.bass as bass
import concourse.mybir as mybir
from concourse.bass_utils import run_bass_kernel_spmd

F32 = mybir.dt.float32
BF16 = mybir.dt.bfloat16
U8 = mybir.dt.uint8
I32 = mybir.dt.int32
AF = mybir.ActivationFunctionType
ALU = mybir.AluOpType
AX = mybir.AxisListType

B, L, D = 4, 4096, 2048
HID = 5632
NCH = 32
T = 2048
NT = 16
HB = 44
TH = 1024
PI = float(np.pi)
TWO_PI = float(2.0 * np.pi)
EPS_LN = 1e-5
KB = 1024


class Ctx:

    KEYS = ("d_par", "d_u", "d_x0", "d_x1w", "d_x1r", "d_xnw", "d_xnr",
            "d_w", "d_yw", "d_yr", "d_out", "v", "s", "t", "g")

    def __init__(self):
        self.marks = {}
        self.pass_num = 0
        self.cnt = {}
        self.sems = None

    def begin(self, pass_num, sems=None):
        self.pass_num = pass_num
        self.cnt = {k: 0 for k in self.KEYS}
        self.sems = sems

    def inc(self, inst, key, by=1, mark=None):
        if self.pass_num == 2:
            inst.then_inc(self.sems[key], by)
        self.cnt[key] += by
        if mark is not None:
            if self.pass_num == 1:
                assert mark not in self.marks, f"dup mark {mark}"
                self.marks[mark] = (key, self.cnt[key])
            else:
                assert self.marks[mark] == (key, self.cnt[key]), \
                    f"mark drift {mark}"
        return self.cnt[key]

    def wait(self, eng, mark, delta=0):
        if self.pass_num == 2:
            key, val = self.marks[mark]
            if val + delta > 0:
                eng.wait_ge(self.sems[key], val + delta)

    def wait_cnt(self, eng, key, val):
        if self.pass_num == 2 and val > 0:
            eng.wait_ge(self.sems[key], val)

    def selfwait(self, eng, key):
        if self.pass_num == 2 and self.cnt[key] > 0:
            eng.wait_ge(self.sems[key], self.cnt[key])


class _FakeInst:
    def then_inc(self, *a, **k):
        return self


class _FakeEng:
    def __getattr__(self, name):
        return lambda *a, **k: _FakeInst()


def build_program():
    nc = bass.Bass("TRN2", target_bir_lowering=False, debug=False)

    d_in = {}

    def din(name, shape, dt):
        d_in[name] = nc.dram_tensor(name, shape, dt, kind="ExternalInput").ap()

    din("u", [T, D], F32)
    din("uprev", [T, D], F32)
    din("wg", [HB, 128, D], U8)
    din("wu", [HB, 128, D], U8)
    din("wd", [HB, 128, D], U8)
    din("sg", [128, HB], F32)
    din("su", [128, HB], F32)
    din("sd", [1, D], F32)
    din("bre", [NCH, D], F32)
    din("bim", [NCH, D], F32)
    din("cret", [NCH, D], F32)
    din("cimt", [NCH, D], F32)
    din("drow", [1, D], F32)
    din("lunc", [NCH, 1], F32)
    din("lim", [NCH, 1], F32)
    din("logd", [NCH, 1], F32)
    out_hbm = nc.dram_tensor("out", [T, D], F32, kind="ExternalOutput").ap()
    x1_spill = nc.dram_tensor("x1sp", [T, D], F32, kind="Internal").ap()
    xnt_spill = nc.dram_tensor("xntsp", [128, 16 * TH], BF16,
                               kind="Internal").ap()
    y_spill = nc.dram_tensor("ysp", [TH, D], F32, kind="Internal").ap()

    st = contextlib.ExitStack()
    ARENA = 207 * KB
    arena = st.enter_context(nc.sbuf_tensor("arena", [128, ARENA], U8)).ap()

    def vf32(off_kb, n):
        off = int(off_kb * KB)
        assert off % 4 == 0 and off + 4 * n <= ARENA, (off_kb, n)
        return arena[:, off:off + 4 * n].bitcast(F32)

    def vbf(off_kb, n):
        off = int(off_kb * KB)
        assert off % 2 == 0 and off + 2 * n <= ARENA, (off_kb, n)
        return arena[:, off:off + 2 * n].bitcast(BF16)

    def vi32(off_kb, n):
        off = int(off_kb * KB)
        return arena[:, off:off + 4 * n].bitcast(I32)

    t1n_full = vbf(0, 32768)
    wd_bf = vbf(0, 22528)
    wgbuf = [vbf(44, 2048), vbf(48, 2048)]
    wubuf = [vbf(52, 2048), vbf(56, 2048)]
    us_s1 = vbf(60, 1024)
    yrbuf = vf32(0, 2048)
    x1rbuf = vf32(8, 2048)
    outbuf = [vf32(16, 2048), vf32(24, 2048)]

    xnt = vbf(64, 16384)
    drow_f = vf32(70, 2048)[0:1, :]
    sdrow = vf32(78, 2048)[0:1, :]
    drow_bf = vbf(86, 2048)[0:1, :]

    ubuf = [vf32(96, 2048), vf32(104, 2048)]
    t1nT = [vbf(112, 2048), vbf(116, 2048)]
    bu_re_m = vf32(120, 2048)[0:NCH, :]
    bu_im_m = vf32(128, 2048)[0:NCH, :]
    bu_re_p = vf32(136, 2048)[0:NCH, :]
    bu_im_p = vf32(144, 2048)[0:NCH, :]
    cosT = vf32(152, 2048)[0:NCH, :]
    sinT = vf32(160, 2048)[0:NCH, :]
    it_fwd = vf32(168, 2048)[0:NCH, :]
    rp_v = vf32(168, 2048)[0:NCH, :]
    c2re_v = vbf(176, 2048)[0:NCH, :]
    c2im_v = vbf(180, 2048)[0:NCH, :]
    bbar_sA = vf32(96, 2048)[0:NCH, :]
    bbar_sB = vf32(104, 2048)[0:NCH, :]
    bbar_re = vbf(112, 2048)[0:NCH, :]
    bbar_im = vbf(116, 2048)[0:NCH, :]
    ph_p = vf32(120, 2048)[0:NCH, :]
    ph_q = vf32(128, 2048)[0:NCH, :]
    ph_qi = vi32(136, 2048)[0:NCH, :]
    ph_r = vf32(144, 2048)[0:NCH, :]
    tA = vf32(96, 2048)[0:NCH, :]
    tB = vf32(104, 2048)[0:NCH, :]
    tC = vf32(112, 2048)[0:NCH, :]
    bt_re = vf32(136, 2048)[0:NCH, :]
    bt_im = vf32(144, 2048)[0:NCH, :]
    xs_re = vbf(120, 2048)[0:NCH, :]
    xs_im = vbf(124, 2048)[0:NCH, :]
    tmpD = vf32(96, 2048)
    ybuf = vf32(104, 2048)
    x1buf = [vf32(112, 2048), vf32(128, 2048)]
    x0buf = [vf32(136, 2048), vf32(144, 2048)]
    xnbuf = vbf(152, 2048)
    xnt_stage = vbf(156, 2048)
    Hbuf = vbf(96, 45056)

    d_rep_bf = vbf(184, 2048)
    sdrep = vf32(188, 2048)

    id_bf = vbf(196, 128)
    onesrow_bf = vbf(196.25, 128)[0:1, :]
    onesrow_f = vf32(196.5, 128)[0:1, :]
    sgbuf = vf32(197, HB)
    subuf = vf32(197.25, HB)
    statbuf = vf32(197.5, 24)
    mvbuf = vf32(197.75, 2)
    colbuf = vf32(198, 32)
    bret_bf = vbf(198.25, 512)
    bimt_bf = vbf(199.25, 512)
    ystage = [vf32(200.25, 512), vf32(202.25, 512)]
    ln4stats = vf32(204.25, 192)
    h2buf = vbf(205, 1024)

    pA = st.enter_context(nc.psum_tensor("pA", [128, 2048], F32)).ap()
    pB = st.enter_context(nc.psum_tensor("pB", [128, 1024], F32)).ap()
    pC = st.enter_context(nc.psum_tensor("pC", [128, 512], F32)).ap()
    pD = st.enter_context(nc.psum_tensor("pD", [128, 512], F32)).ap()
    pBbf = pB.bitcast(BF16)

    sems = {k: st.enter_context(nc.semaphore(k)) for k in Ctx.KEYS}
    C = Ctx()

    p_lunc = colbuf[0:NCH, 0:1]
    p_lim = colbuf[0:NCH, 1:2]
    p_logd = colbuf[0:NCH, 2:3]
    c_delta = colbuf[0:NCH, 3:4]
    c_lamre = colbuf[0:NCH, 4:5]
    c_lnr = colbuf[0:NCH, 5:6]
    c_theta = colbuf[0:NCH, 6:7]
    c_r = colbuf[0:NCH, 7:8]
    c_are = colbuf[0:NCH, 8:9]
    c_aim = colbuf[0:NCH, 9:10]
    c_t1 = colbuf[0:NCH, 10:11]
    c_t2 = colbuf[0:NCH, 11:12]
    c_t3 = colbuf[0:NCH, 12:13]
    c_t4 = colbuf[0:NCH, 13:14]
    c_dre = colbuf[0:NCH, 14:15]
    c_dim = colbuf[0:NCH, 15:16]
    c_eps = colbuf[:, 16:17]
    c_cre = colbuf[0:NCH, 17:18]
    c_cim = colbuf[0:NCH, 18:19]
    c_sr = colbuf[0:NCH, 19:20]
    c_si = colbuf[0:NCH, 20:21]
    c_negm = colbuf[:, 21:22]
    c_rstd = colbuf[:, 22:23]
    c_sd = colbuf[:, 23:24]
    c_Tlnr = colbuf[0:NCH, 24:25]
    c_i32 = colbuf.bitcast(I32)[0:NCH, 25:26]
    c_sinth = colbuf[0:NCH, 26:27]
    c_sh = colbuf[0:NCH, 27:28]
    c_costh = colbuf[0:NCH, 28:29]
    c_thred = colbuf[0:NCH, 29:30]
    p_bre = bu_re_m
    p_bim = bu_im_m
    p_cret = cosT
    p_cimt = sinT

    def Hcols(hb, tg):
        return Hbuf[:, hb * TH + tg * 512: hb * TH + (tg + 1) * 512]

    def emit_sync(q):
        C.inc(q.dma_start(p_lunc, d_in["lunc"]), "d_par", 16)
        C.inc(q.dma_start(p_lim, d_in["lim"]), "d_par", 16)
        C.inc(q.dma_start(p_logd, d_in["logd"]), "d_par", 16)
        C.inc(q.dma_start(p_bre, d_in["bre"]), "d_par", 16)
        C.inc(q.dma_start(p_bim, d_in["bim"]), "d_par", 16)
        C.inc(q.dma_start(p_cret, d_in["cret"]), "d_par", 16)
        C.inc(q.dma_start(p_cimt, d_in["cimt"]), "d_par", 16)
        C.inc(q.dma_start(drow_f, d_in["drow"]), "d_par", 16)
        C.inc(q.dma_start(sdrow, d_in["sd"]), "d_par", 16)
        C.inc(q.dma_start(sgbuf[:, :], d_in["sg"]), "d_par", 16)
        C.inc(q.dma_start(subuf[:, :], d_in["su"]), "d_par", 16,
              mark="d_params")

        for j in range(2 * NT):
            src = d_in["uprev"] if j < NT else d_in["u"]
            i = j % NT
            if j < 2:
                C.wait(q, "v_bbar")
            else:
                C.wait(q, f"uapply_{j-2}")
            C.inc(q.dma_start(ubuf[j % 2][:, :],
                              src[i * 128:(i + 1) * 128, :]),
                  "d_u", 16, mark=f"du_{j}")

        for i in range(NT):
            if i < 2:
                C.wait(q, "v_scan_done")
            else:
                C.wait(q, f"x1_{i-2}")
            C.inc(q.dma_start(x0buf[i % 2][:, :],
                              d_in["u"][i * 128:(i + 1) * 128, :]),
                  "d_x0", 16, mark=f"dx0_{i}")
            if i >= 1:
                C.wait(q, f"x1_{i-1}")
                C.inc(q.dma_start(x1_spill[(i - 1) * 128:i * 128, :],
                                  x1buf[(i - 1) % 2][:, :]), "d_x1w", 16,
                      mark=f"dx1w_{i-1}")
            if i >= 9:
                C.wait(q, f"xnev_{i-1}")
                C.inc(q.dma_start(
                    xnt_spill.rearrange("p (k ti tt) -> p k ti tt",
                                        k=16, ti=8)[:, :, i - 9, :],
                    xnt_stage.rearrange("p (k tt) -> p k tt", k=16)),
                    "d_xnw", 16, mark=f"dxnw_{i-9}")
        C.wait(q, f"x1_{NT-1}")
        C.inc(q.dma_start(x1_spill[(NT - 1) * 128:NT * 128, :],
                          x1buf[(NT - 1) % 2][:, :]), "d_x1w", 16,
              mark=f"dx1w_{NT-1}")
        C.wait(q, f"xnev_{NT-1}")
        C.inc(q.dma_start(
            xnt_spill.rearrange("p (k ti tt) -> p k ti tt", k=16, ti=8)[
                :, :, 7, :],
            xnt_stage.rearrange("p (k tt) -> p k tt", k=16)),
            "d_xnw", 16, mark="dxnw_7")

        for X in range(2):
            if X == 1:
                C.wait(q, "s1mm_0_43")
                C.inc(q.dma_start(xnt[:, :], xnt_spill[:, :]), "d_xnr", 16,
                      mark="dxnr")
            for dg in range(4):
                for ti in range(8):
                    par = (dg * 8 + ti) % 2
                    C.wait(q, f"ys_{X}_{dg}_{ti}")
                    C.inc(q.dma_start(
                        y_spill[ti * 128:(ti + 1) * 128,
                                dg * 512:(dg + 1) * 512],
                        ystage[par][:, :]), "d_yw", 16,
                        mark=f"dyw_{X}_{dg}_{ti}")
            C.wait(q, f"s2mm_{X}_3_7")
            for ti in range(8):
                gt = X * 8 + ti
                if ti >= 1:
                    C.wait(q, f"ln4apply_{X}_{ti-1}")
                C.inc(q.dma_start(yrbuf[:, :],
                                  y_spill[ti * 128:(ti + 1) * 128, :]),
                      "d_yr", 16, mark=f"dyr_{X}_{ti}")
                if ti >= 1:
                    C.wait(q, f"out_{X}_{ti-1}")
                C.inc(q.dma_start(x1rbuf[:, :],
                                  x1_spill[gt * 128:(gt + 1) * 128, :]),
                      "d_x1r", 16, mark=f"dx1r_{X}_{ti}")
                C.wait(q, f"out_{X}_{ti}")
                C.inc(q.dma_start(out_hbm[gt * 128:(gt + 1) * 128, :],
                                  outbuf[ti % 2][:, :]), "d_out", 16,
                      mark=f"dout_{X}_{ti}")
        C.wait_cnt(q, "d_out", 16 * 16)

    def emit_gpsimd(g):
        C.inc(g.iota(it_fwd[:, :], pattern=[[1, T]], base=1,
                     channel_multiplier=0,
                     allow_small_or_imprecise_dtypes=True), "g", mark="g_itf")
        C.inc(g.memset(id_bf[:, :], 1.0), "g")
        C.selfwait(g, "g")
        C.inc(g.affine_select(id_bf[:, :], id_bf[:, :], pattern=[[-1, 128]],
                              compare_op=ALU.is_equal, fill=0.0, base=0,
                              channel_multiplier=1), "g", mark="g_id")
        C.inc(g.memset(onesrow_bf[:, :], 1.0), "g")
        C.inc(g.memset(onesrow_f[:, :], 1.0), "g")
        C.inc(g.memset(c_eps, EPS_LN), "g", mark="g_setup")

        for X in range(2):
            for hb in range(HB):
                par = hb % 2
                if X == 0 and hb < 2:
                    C.wait(g, "yev_15")
                else:
                    pX, phb = (X, hb - 2) if hb >= 2 else (0, HB - 2 + hb)
                    C.wait(g, f"s1mm_{pX}_{phb}")
                C.inc(g.dma_start(wgbuf[par][:, :], d_in["wg"][hb, :, :]),
                      "d_w", 16)
                C.inc(g.dma_start(wubuf[par][:, :], d_in["wu"][hb, :, :]),
                      "d_w", 16, mark=f"wconv_{X}_{hb}")
            for dg in range(4):
                if dg == 0:
                    C.wait(g, f"h_{X}_{HB-1}")
                    if X == 1:
                        C.wait_cnt(g, "d_out", 16 * 8)
                else:
                    C.wait(g, f"s2mm_{X}_{dg-1}_7")
                C.inc(g.dma_start(
                    wd_bf.rearrange("p (h dd) -> p h dd", h=HB),
                    d_in["wd"][:, :, dg * 512:(dg + 1) * 512].rearrange(
                        "h p dd -> p h dd")), "d_w", 16, mark=f"dwd_{X}_{dg}")

    def emit_scalar(s):
        C.wait(s, "d_params")
        C.inc(s.activation(c_delta, p_logd, AF.Exp), "s")
        C.inc(s.activation(c_t1, p_lunc, AF.Abs), "s")
        C.selfwait(s, "s")
        C.inc(s.activation(c_t2, c_t1, AF.Exp, scale=-1.0), "s")
        C.selfwait(s, "s")
        C.inc(s.activation(c_t3, c_t2, AF.Ln, bias=1.0), "s", mark="s_p1")
        C.wait(s, "v_lnr")
        C.inc(s.activation(c_r, c_lnr, AF.Exp), "s", mark="s_r")
        C.wait(s, "v_thred")
        C.inc(s.activation(c_sinth, c_thred, AF.Sin), "s")
        C.inc(s.activation(c_sh, c_thred, AF.Sin, scale=0.5), "s",
              mark="s_sincos")
        C.wait(s, "v_pred")
        C.inc(s.activation(sinT[:, :], ph_p[:, :], AF.Sin), "s")
        C.inc(s.activation(ph_q[:, :], ph_p[:, :], AF.Abs), "s",
              mark="s_abs")
        C.wait(s, "v_carg")
        C.inc(s.activation(cosT[:, :], ph_r[:, :], AF.Sin), "s",
              mark="s_tables")
        C.inc(s.activation(drow_bf, drow_f, AF.Copy), "s", mark="s_dbf")

        for j in range(2 * NT):
            C.wait(s, f"aggr_{j}")
            C.inc(s.activation(c_sd, mvbuf[:, 1:2], AF.Sqrt, bias=c_eps),
                  "s", mark=f"sqrt_{j}")
        C.wait(s, "v_rpe")
        C.inc(s.activation(rp_v[:, :], tC[:, :], AF.Exp, scale=-1.0,
                           bias=c_Tlnr), "s", mark="s_rpow")
        for i in range(NT):
            C.wait(s, f"yaggr_{i}")
            C.inc(s.activation(c_sd, mvbuf[:, 1:2], AF.Sqrt, bias=c_eps),
                  "s", mark=f"ysqrt_{i}")
            C.wait(s, f"x1aggr_{i}")
            C.inc(s.activation(c_sd, mvbuf[:, 1:2], AF.Sqrt, bias=c_eps),
                  "s", mark=f"x1sqrt_{i}")
        for X in range(2):
            for hb in range(HB):
                C.wait(s, f"gmm_{X}_{hb}")
                if hb >= 1 or X == 1:
                    pX, phb = (X, hb - 1) if hb >= 1 else (0, HB - 1)
                    C.wait(s, f"h_{pX}_{phb}")
                for tg in range(2):
                    C.inc(s.activation(h2buf[:, tg * 512:(tg + 1) * 512],
                                       pB[:, tg * 512:(tg + 1) * 512],
                                       AF.Silu, scale=sgbuf[:, hb:hb + 1]),
                          "s", mark=(f"gs_{X}_{hb}" if tg == 1 else None))
            for ti in range(8):
                C.wait(s, f"l4aggr_{X}_{ti}")
                C.inc(s.activation(c_sd, mvbuf[:, 1:2], AF.Sqrt, bias=c_eps),
                      "s", mark=f"l4sqrt_{X}_{ti}")

    def emit_tensor(t):
        C.wait(t, "g_id")
        C.wait(t, "v_bbar")
        for kt in range(16):
            C.inc(t.transpose(pBbf[:, kt * 32:(kt + 1) * 32],
                              bbar_re[:, kt * 128:(kt + 1) * 128],
                              id_bf[0:NCH, 0:NCH]), "t")
        for kt in range(16):
            C.inc(t.transpose(pBbf[:, 512 + kt * 32:512 + (kt + 1) * 32],
                              bbar_im[:, kt * 128:(kt + 1) * 128],
                              id_bf[0:NCH, 0:NCH]), "t",
                  mark=("t_btr" if kt == 15 else None))
        C.wait(t, "s_dbf")
        for dg in range(4):
            C.inc(t.matmul(pA[:, dg * 512:(dg + 1) * 512], onesrow_bf,
                           drow_bf[:, dg * 512:(dg + 1) * 512],
                           start=True, stop=True), "t",
                  mark=("t_drep" if dg == 3 else None))
        C.wait(t, "v_drep")
        for dg in range(4):
            C.inc(t.matmul(pA[:, dg * 512:(dg + 1) * 512], onesrow_f,
                           sdrow[:, dg * 512:(dg + 1) * 512],
                           start=True, stop=True), "t",
                  mark=("t_sdrep" if dg == 3 else None))

        for j in range(2 * NT):
            C.wait(t, f"uapply_{j}")
            if j >= 1:
                C.wait(t, f"t1nT_{j-1}")
            i = j % NT
            src = t1n_full[:, i * 2048:(i + 1) * 2048]
            for kt in range(16):
                C.inc(t.transpose(pBbf[:, kt * 128:(kt + 1) * 128],
                                  src[:, kt * 128:(kt + 1) * 128],
                                  id_bf[:, :]), "t",
                      mark=(f"tr_{j}" if kt == 15 else None))
            C.wait(t, f"t1nT_{j}")
            if j >= 1:
                C.wait(t, f"buev_{j-1}")
            for kt in range(16):
                C.inc(t.matmul(pC[0:NCH, 0:128],
                               bret_bf[:, kt * 32:(kt + 1) * 32],
                               t1nT[j % 2][:, kt * 128:(kt + 1) * 128],
                               start=(kt == 0), stop=(kt == 15)), "t")
            for kt in range(16):
                C.inc(t.matmul(pD[0:NCH, 0:128],
                               bimt_bf[:, kt * 32:(kt + 1) * 32],
                               t1nT[j % 2][:, kt * 128:(kt + 1) * 128],
                               start=(kt == 0), stop=(kt == 15)), "t",
                      mark=(f"bu_{j}" if kt == 15 else None))

        C.wait(t, "v_scan_done")
        C.wait(t, "v_sdrep")
        for i in range(NT):
            if i >= 1:
                C.wait(t, f"yev_{i-1}")
            for dg in range(4):
                C.inc(t.matmul(pA[:, dg * 512:(dg + 1) * 512],
                               xs_re[:, i * 128:(i + 1) * 128],
                               c2re_v[:, dg * 512:(dg + 1) * 512],
                               start=True, stop=False), "t")
                C.inc(t.matmul(pA[:, dg * 512:(dg + 1) * 512],
                               xs_im[:, i * 128:(i + 1) * 128],
                               c2im_v[:, dg * 512:(dg + 1) * 512],
                               start=False, stop=True), "t",
                      mark=(f"ymm_{i}" if dg == 3 else None))
            C.wait(t, f"xnapply_{i}")
            if i == 0:
                C.wait(t, f"buev_{2*NT-1}")
            else:
                C.wait(t, f"xnev_{i-1}")
            for kt in range(16):
                C.inc(t.transpose(pBbf[:, kt * 128:(kt + 1) * 128],
                                  xnbuf[:, kt * 128:(kt + 1) * 128],
                                  id_bf[:, :]), "t",
                      mark=(f"xntr_{i}" if kt == 15 else None))

        for X in range(2):
            for hb in range(HB):
                par = hb % 2
                C.wait(t, f"wconv_{X}_{hb}")
                if X == 0 and hb == 0:
                    C.wait(t, "xnev_15")
                if X == 1 and hb == 0:
                    C.wait(t, "dxnr")
                if hb >= 1 or X == 1:
                    pX, phb = (X, hb - 1) if hb >= 1 else (0, HB - 1)
                    C.wait(t, f"gs_{pX}_{phb}")
                    C.wait(t, f"uev_{pX}_{phb}")
                for tg in range(2):
                    for kt in range(16):
                        C.inc(t.matmul(
                            pB[:, tg * 512:(tg + 1) * 512],
                            wgbuf[par][:, kt * 128:(kt + 1) * 128],
                            xnt[:, kt * TH + tg * 512:
                                kt * TH + (tg + 1) * 512],
                            start=(kt == 0), stop=(kt == 15)), "t",
                            mark=(f"gmm_{X}_{hb}"
                                  if tg == 1 and kt == 15 else None))
                for tg in range(2):
                    ps = pC if tg == 0 else pD
                    for kt in range(16):
                        C.inc(t.matmul(
                            ps[:, :],
                            wubuf[par][:, kt * 128:(kt + 1) * 128],
                            xnt[:, kt * TH + tg * 512:
                                kt * TH + (tg + 1) * 512],
                            start=(kt == 0), stop=(kt == 15)), "t",
                            mark=(f"s1mm_{X}_{hb}"
                                  if tg == 1 and kt == 15 else None))
            for dg in range(4):
                C.wait(t, f"wdconv_{X}_{dg}")
                for ti in range(8):
                    ps = pC if ti % 2 == 0 else pD
                    k = dg * 8 + ti
                    if k >= 2:
                        pk = k - 2
                        C.wait(t, f"ys_{X}_{pk // 8}_{pk % 8}")
                    else:
                        C.wait(t, f"uev_{X}_{HB-1}")
                    for hb in range(HB):
                        C.inc(t.matmul(
                            ps[:, :],
                            Hbuf[:, hb * TH + ti * 128:
                                 hb * TH + (ti + 1) * 128],
                            wd_bf[:, hb * 512:(hb + 1) * 512],
                            start=(hb == 0), stop=(hb == HB - 1)), "t",
                            mark=(f"s2mm_{X}_{dg}_{ti}"
                                  if hb == HB - 1 else None))

    def emit_vector(v):
        C.wait(v, "d_params")
        C.wait(v, "g_setup")
        C.inc(v.tensor_scalar(c2re_v[:, :], p_cret, 2.0, None, ALU.mult),
              "v")
        C.inc(v.tensor_scalar(c2im_v[:, :], p_cimt, -2.0, None, ALU.mult),
              "v", mark="v_c2")
        C.wait(v, "s_p1")
        C.inc(v.tensor_scalar(c_t1, p_lunc, 0.0, None, ALU.max), "v")
        C.selfwait(v, "v")
        C.inc(v.tensor_tensor(c_t1, c_t1, c_t3, ALU.add), "v")
        C.selfwait(v, "v")
        C.inc(v.tensor_scalar(c_lamre, c_t1, 0.0101, -1.0, ALU.add,
                              ALU.mult), "v")
        C.selfwait(v, "v")
        C.inc(v.tensor_tensor(c_lnr, c_lamre, c_delta, ALU.mult), "v",
              mark="v_lnr")
        C.selfwait(v, "v")
        C.inc(v.tensor_scalar(c_Tlnr, c_lnr, float(T), None, ALU.mult), "v")
        C.inc(v.tensor_tensor(c_theta, p_lim, c_delta, ALU.mult), "v")
        C.selfwait(v, "v")
        C.inc(v.tensor_scalar(c_t2, c_theta, 1.0 / TWO_PI, None, ALU.mult),
              "v")
        C.selfwait(v, "v")
        C.inc(v.tensor_copy(c_i32, c_t2), "v")
        C.selfwait(v, "v")
        C.inc(v.tensor_copy(c_t2, c_i32), "v")
        C.selfwait(v, "v")
        C.inc(v.scalar_tensor_tensor(c_thred, c_t2, -TWO_PI, c_theta,
                                     ALU.mult, ALU.add), "v", mark="v_thred")
        C.wait(v, "s_sincos")
        C.inc(v.tensor_tensor(c_t2, c_sh, c_sh, ALU.mult), "v")
        C.selfwait(v, "v")
        C.inc(v.tensor_scalar(c_costh, c_t2, -2.0, 1.0, ALU.mult, ALU.add),
              "v")
        C.wait(v, "s_r")
        C.selfwait(v, "v")
        C.inc(v.tensor_tensor(c_are, c_r, c_costh, ALU.mult), "v")
        C.inc(v.tensor_tensor(c_aim, c_r, c_sinth, ALU.mult), "v")
        C.selfwait(v, "v")
        C.inc(v.tensor_scalar(c_t1, c_are, -1.0, None, ALU.add), "v")
        C.inc(v.tensor_tensor(c_t2, c_lamre, c_lamre, ALU.mult), "v")
        C.inc(v.tensor_tensor(c_t3, p_lim, p_lim, ALU.mult), "v")
        C.selfwait(v, "v")
        C.inc(v.tensor_tensor(c_t2, c_t2, c_t3, ALU.add), "v")
        C.selfwait(v, "v")
        C.inc(v.reciprocal(c_t2, c_t2), "v")
        C.selfwait(v, "v")
        C.inc(v.tensor_tensor(c_t3, c_t1, c_lamre, ALU.mult), "v")
        C.inc(v.tensor_tensor(c_t4, c_aim, p_lim, ALU.mult), "v")
        C.selfwait(v, "v")
        C.inc(v.tensor_tensor(c_t3, c_t3, c_t4, ALU.add), "v")
        C.selfwait(v, "v")
        C.inc(v.tensor_tensor(c_dre, c_t3, c_t2, ALU.mult), "v")
        C.inc(v.tensor_tensor(c_t3, c_aim, c_lamre, ALU.mult), "v")
        C.inc(v.tensor_tensor(c_t4, c_t1, p_lim, ALU.mult), "v")
        C.selfwait(v, "v")
        C.inc(v.tensor_tensor(c_t3, c_t3, c_t4, ALU.subtract), "v")
        C.selfwait(v, "v")
        C.inc(v.tensor_tensor(c_dim, c_t3, c_t2, ALU.mult), "v")
        C.selfwait(v, "v")
        C.inc(v.tensor_scalar(bbar_sA[:, :], p_bre, c_dre, None, ALU.mult),
              "v")
        C.inc(v.tensor_scalar(bbar_sB[:, :], p_bim, c_dim, None, ALU.mult),
              "v")
        C.selfwait(v, "v")
        C.inc(v.tensor_tensor(bbar_re[:, :], bbar_sA[:, :], bbar_sB[:, :],
                              ALU.subtract), "v")
        C.inc(v.tensor_scalar(bbar_sA[:, :], p_bim, c_dre, None, ALU.mult),
              "v")
        C.inc(v.tensor_scalar(bbar_sB[:, :], p_bre, c_dim, None, ALU.mult),
              "v")
        C.selfwait(v, "v")
        C.inc(v.tensor_tensor(bbar_im[:, :], bbar_sA[:, :], bbar_sB[:, :],
                              ALU.add), "v", mark="v_bbar")
        C.inc(v.tensor_scalar(ph_p[:, :], it_fwd[:, :], c_theta, None,
                              ALU.mult), "v")
        C.selfwait(v, "v")
        C.inc(v.tensor_scalar(ph_q[:, :], ph_p[:, :], 1.0 / TWO_PI, None,
                              ALU.mult), "v")
        C.selfwait(v, "v")
        C.inc(v.tensor_copy(ph_qi, ph_q[:, :]), "v")
        C.selfwait(v, "v")
        C.inc(v.tensor_copy(ph_r[:, :], ph_qi), "v")
        C.selfwait(v, "v")
        C.inc(v.scalar_tensor_tensor(ph_p[:, :], ph_r[:, :], -TWO_PI,
                                     ph_p[:, :], ALU.mult, ALU.add), "v",
              mark="v_pred")
        C.wait(v, "s_abs")
        C.inc(v.tensor_scalar(ph_r[:, :], ph_q[:, :], -1.0, PI / 2.0,
                              ALU.mult, ALU.add), "v", mark="v_carg")
        C.wait(v, "t_btr")
        C.inc(v.tensor_copy(bret_bf[:, :], pBbf[:, 0:512]), "v")
        C.inc(v.tensor_copy(bimt_bf[:, :], pBbf[:, 512:1024]), "v",
              mark="v_btrev")
        C.wait(v, "t_drep")
        C.inc(v.tensor_copy(d_rep_bf[:, :], pA[:, :]), "v", mark="v_drep")
        C.wait(v, "t_sdrep")
        C.inc(v.tensor_copy(sdrep[:, :], pA[:, :]), "v", mark="v_sdrep")

        for j in range(2 * NT):
            i = j % NT
            C.wait(v, f"du_{j}")
            for gq in range(4):
                C.inc(v.bn_stats(statbuf[:, 6 * gq:6 * (gq + 1)],
                                 ubuf[j % 2][:, 512 * gq:512 * (gq + 1)]),
                      "v")
            C.selfwait(v, "v")
            C.inc(v.bn_aggr(mvbuf[:, :], statbuf[:, :]), "v",
                  mark=f"aggr_{j}")
            C.wait(v, f"sqrt_{j}")
            C.inc(v.reciprocal(c_rstd, c_sd), "v")
            C.inc(v.tensor_scalar(c_negm, mvbuf[:, 0:1], -1.0, None,
                                  ALU.mult), "v")
            C.selfwait(v, "v")
            C.inc(v.tensor_scalar(t1n_full[:, i * 2048:(i + 1) * 2048],
                                  ubuf[j % 2][:, :], c_negm, c_rstd,
                                  ALU.add, ALU.mult), "v",
                  mark=f"uapply_{j}")
            C.wait(v, f"tr_{j}")
            for gq in range(4):
                C.inc(v.tensor_copy(t1nT[j % 2][:, 512 * gq:512 * (gq + 1)],
                                    pBbf[:, 512 * gq:512 * (gq + 1)]), "v",
                      mark=(f"t1nT_{j}" if gq == 3 else None))
            C.wait(v, f"bu_{j}")
            bre_dst = (bu_re_p if j < NT else bu_re_m)[
                :, i * 128:(i + 1) * 128]
            bim_dst = (bu_im_p if j < NT else bu_im_m)[
                :, i * 128:(i + 1) * 128]
            C.inc(v.tensor_copy(bre_dst, pC[0:NCH, 0:128]), "v")
            C.inc(v.tensor_copy(bim_dst, pD[0:NCH, 0:128]), "v",
                  mark=f"buev_{j}")

        C.inc(v.tensor_scalar(tC[:, :], it_fwd[:, :], c_lnr, None, ALU.mult),
              "v", mark="v_rpe")
        C.wait(v, "s_rpow")
        C.inc(v.tensor_tensor(tA[:, :], cosT[:, :], rp_v[:, :], ALU.mult),
              "v")
        C.inc(v.tensor_tensor(tB[:, :], sinT[:, :], rp_v[:, :], ALU.mult),
              "v")
        C.selfwait(v, "v")
        C.inc(v.tensor_tensor(tC[:, :], tA[:, :], bu_re_p[:, :], ALU.mult),
              "v")
        C.selfwait(v, "v")
        C.inc(v.reduce_sum(c_t1, tC[:, :], axis=AX.X), "v")
        C.inc(v.tensor_tensor(tC[:, :], tB[:, :], bu_im_p[:, :], ALU.mult),
              "v")
        C.selfwait(v, "v")
        C.inc(v.reduce_sum(c_t2, tC[:, :], axis=AX.X), "v")
        C.inc(v.tensor_tensor(tC[:, :], tA[:, :], bu_im_p[:, :], ALU.mult),
              "v")
        C.selfwait(v, "v")
        C.inc(v.reduce_sum(c_t3, tC[:, :], axis=AX.X), "v")
        C.inc(v.tensor_tensor(tC[:, :], tB[:, :], bu_re_p[:, :], ALU.mult),
              "v")
        C.selfwait(v, "v")
        C.inc(v.reduce_sum(c_t4, tC[:, :], axis=AX.X), "v")
        C.selfwait(v, "v")
        C.inc(v.tensor_tensor(c_sr, c_t1, c_t2, ALU.add), "v")
        C.inc(v.tensor_tensor(c_si, c_t3, c_t4, ALU.subtract), "v")
        C.selfwait(v, "v")
        C.inc(v.tensor_tensor(c_t1, cosT[:, T - 1:T], c_sr, ALU.mult), "v")
        C.inc(v.tensor_tensor(c_t2, sinT[:, T - 1:T], c_si, ALU.mult), "v")
        C.inc(v.tensor_tensor(c_t3, sinT[:, T - 1:T], c_sr, ALU.mult), "v")
        C.inc(v.tensor_tensor(c_t4, cosT[:, T - 1:T], c_si, ALU.mult), "v")
        C.selfwait(v, "v")
        C.inc(v.tensor_tensor(c_cre, c_t1, c_t2, ALU.subtract), "v")
        C.inc(v.tensor_tensor(c_cim, c_t3, c_t4, ALU.add), "v")
        C.selfwait(v, "v")
        C.inc(v.tensor_tensor(tA[:, :], cosT[:, :], bu_re_m[:, :], ALU.mult),
              "v")
        C.inc(v.tensor_tensor(tB[:, :], sinT[:, :], bu_im_m[:, :], ALU.mult),
              "v")
        C.selfwait(v, "v")
        C.inc(v.tensor_tensor(bt_re[:, :], tA[:, :], tB[:, :], ALU.add), "v")
        C.inc(v.tensor_tensor(tA[:, :], cosT[:, :], bu_im_m[:, :], ALU.mult),
              "v")
        C.inc(v.tensor_tensor(tB[:, :], sinT[:, :], bu_re_m[:, :], ALU.mult),
              "v")
        C.selfwait(v, "v")
        C.inc(v.tensor_tensor(bt_im[:, :], tA[:, :], tB[:, :], ALU.subtract),
              "v")
        C.inc(v.memset(tC[:, :], 1.0), "v")
        C.selfwait(v, "v")
        C.inc(v.tensor_scalar(tC[:, :], tC[:, :], c_r, None, ALU.mult), "v")
        C.selfwait(v, "v")
        C.inc(v.tensor_tensor_scan(tA[:, :], tC[:, :], bt_re[:, :], c_cre,
                                   ALU.mult, ALU.add), "v")
        C.inc(v.tensor_tensor_scan(tB[:, :], tC[:, :], bt_im[:, :], c_cim,
                                   ALU.mult, ALU.add), "v")
        C.selfwait(v, "v")
        C.inc(v.tensor_tensor(tC[:, :], cosT[:, :], tA[:, :], ALU.mult), "v")
        C.inc(v.tensor_tensor(bt_re[:, :], sinT[:, :], tB[:, :], ALU.mult),
              "v")
        C.selfwait(v, "v")
        C.inc(v.tensor_tensor(xs_re[:, :], tC[:, :], bt_re[:, :],
                              ALU.subtract), "v")
        C.inc(v.tensor_tensor(tC[:, :], cosT[:, :], tB[:, :], ALU.mult), "v")
        C.inc(v.tensor_tensor(bt_im[:, :], sinT[:, :], tA[:, :], ALU.mult),
              "v")
        C.selfwait(v, "v")
        C.inc(v.tensor_tensor(xs_im[:, :], tC[:, :], bt_im[:, :], ALU.add),
              "v", mark="v_scan_done")

        for i in range(NT):
            C.inc(v.tensor_tensor(tmpD[:, :], d_rep_bf[:, :],
                                  t1n_full[:, i * 2048:(i + 1) * 2048],
                                  ALU.mult), "v")
            C.wait(v, f"ymm_{i}")
            C.selfwait(v, "v")
            for gq in range(4):
                C.inc(v.tensor_tensor(ybuf[:, 512 * gq:512 * (gq + 1)],
                                      pA[:, 512 * gq:512 * (gq + 1)],
                                      tmpD[:, 512 * gq:512 * (gq + 1)],
                                      ALU.add), "v",
                      mark=(f"yev_{i}" if gq == 3 else None))
            C.selfwait(v, "v")
            for gq in range(4):
                C.inc(v.bn_stats(statbuf[:, 6 * gq:6 * (gq + 1)],
                                 ybuf[:, 512 * gq:512 * (gq + 1)]), "v")
            C.selfwait(v, "v")
            C.inc(v.bn_aggr(mvbuf[:, :], statbuf[:, :]), "v",
                  mark=f"yaggr_{i}")
            C.wait(v, f"ysqrt_{i}")
            C.inc(v.reciprocal(c_rstd, c_sd), "v")
            C.inc(v.tensor_scalar(c_negm, mvbuf[:, 0:1], -1.0, None,
                                  ALU.mult), "v")
            C.selfwait(v, "v")
            C.inc(v.tensor_scalar(tmpD[:, :], ybuf[:, :], c_negm, c_rstd,
                                  ALU.add, ALU.mult), "v")
            C.wait(v, f"dx0_{i}")
            if i >= 2:
                C.wait(v, f"dx1w_{i-2}")
            C.selfwait(v, "v")
            C.inc(v.tensor_tensor(x1buf[i % 2][:, :], tmpD[:, :],
                                  x0buf[i % 2][:, :], ALU.add), "v",
                  mark=f"x1_{i}")
            C.selfwait(v, "v")
            for gq in range(4):
                C.inc(v.bn_stats(statbuf[:, 6 * gq:6 * (gq + 1)],
                                 x1buf[i % 2][:, 512 * gq:512 * (gq + 1)]),
                      "v")
            C.selfwait(v, "v")
            C.inc(v.bn_aggr(mvbuf[:, :], statbuf[:, :]), "v",
                  mark=f"x1aggr_{i}")
            C.wait(v, f"x1sqrt_{i}")
            C.inc(v.reciprocal(c_rstd, c_sd), "v")
            C.inc(v.tensor_scalar(c_negm, mvbuf[:, 0:1], -1.0, None,
                                  ALU.mult), "v")
            C.selfwait(v, "v")
            C.inc(v.tensor_scalar(xnbuf[:, :], x1buf[i % 2][:, :], c_negm,
                                  c_rstd, ALU.add, ALU.mult), "v",
                  mark=f"xnapply_{i}")
            C.wait(v, f"xntr_{i}")
            if i < 8:
                for gq in range(4):
                    dst = xnt.rearrange("p (k tt) -> p k tt", k=16)[
                        :, 4 * gq:4 * (gq + 1), i * 128:(i + 1) * 128]
                    C.inc(v.tensor_copy(
                        dst,
                        pBbf[:, 512 * gq:512 * (gq + 1)].rearrange(
                            "p (k tt) -> p k tt", k=4)), "v",
                        mark=(f"xnev_{i}" if gq == 3 else None))
            else:
                if i >= 10:
                    C.wait(v, f"dxnw_{i-10}")
                for gq in range(4):
                    C.inc(v.tensor_copy(
                        xnt_stage[:, 512 * gq:512 * (gq + 1)],
                        pBbf[:, 512 * gq:512 * (gq + 1)]), "v",
                        mark=(f"xnev_{i}" if gq == 3 else None))

        for X in range(2):
            for hb in range(HB):
                if X == 0 and hb == 0:
                    C.wait(v, "dxnw_7")
                    C.wait(v, f"dx1w_{NT-1}")
                C.wait(v, f"s1mm_{X}_{hb}")
                for tg in range(2):
                    ps = pC if tg == 0 else pD
                    C.inc(v.tensor_scalar(
                        us_s1[:, tg * 512:(tg + 1) * 512], ps[:, :],
                        subuf[:, hb:hb + 1], None, ALU.mult), "v",
                        mark=(f"uev_{X}_{hb}" if tg == 1 else None))
                C.wait(v, f"gs_{X}_{hb}")
                C.selfwait(v, "v")
                for tg in range(2):
                    C.inc(v.tensor_tensor(
                        Hcols(hb, tg), h2buf[:, tg * 512:(tg + 1) * 512],
                        us_s1[:, tg * 512:(tg + 1) * 512], ALU.mult), "v",
                        mark=(f"h_{X}_{hb}" if tg == 1 else None))
            for dg in range(4):
                C.wait(v, f"dwd_{X}_{dg}")
                C.inc(v.tensor_scalar(wd_bf[:, :], wd_bf[:, :], -128.0,
                                      None, ALU.add), "v",
                      mark=f"wdconv_{X}_{dg}")
                for ti in range(8):
                    par = (dg * 8 + ti) % 2
                    C.wait(v, f"s2mm_{X}_{dg}_{ti}")
                    k = dg * 8 + ti
                    if k >= 2:
                        pk = k - 2
                        C.wait(v, f"dyw_{X}_{pk // 8}_{pk % 8}")
                    elif X == 1:
                        C.wait(v, f"dyw_0_3_{6 + ti}")
                    ps = pC if ti % 2 == 0 else pD
                    C.inc(v.tensor_tensor(ystage[par][:, :], ps[:, :],
                                          sdrep[:, dg * 512:(dg + 1) * 512],
                                          ALU.mult), "v")
                    C.selfwait(v, "v")
                    C.inc(v.bn_stats(
                        ln4stats[:, ti * 24 + dg * 6:ti * 24 + dg * 6 + 6],
                        ystage[par][:, :]), "v", mark=f"ys_{X}_{dg}_{ti}")
            for ti in range(8):
                C.wait(v, f"dyr_{X}_{ti}")
                C.inc(v.bn_aggr(mvbuf[:, :],
                                ln4stats[:, ti * 24:(ti + 1) * 24]), "v",
                      mark=f"l4aggr_{X}_{ti}")
                C.wait(v, f"l4sqrt_{X}_{ti}")
                C.inc(v.reciprocal(c_rstd, c_sd), "v")
                C.inc(v.tensor_scalar(c_negm, mvbuf[:, 0:1], -1.0, None,
                                      ALU.mult), "v")
                C.selfwait(v, "v")
                C.inc(v.tensor_scalar(yrbuf[:, :], yrbuf[:, :], c_negm,
                                      c_rstd, ALU.add, ALU.mult), "v",
                      mark=f"ln4apply_{X}_{ti}")
                C.wait(v, f"dx1r_{X}_{ti}")
                gk = X * 8 + ti
                if gk >= 2:
                    pgk = gk - 2
                    C.wait(v, f"dout_{pgk // 8}_{pgk % 8}")
                C.selfwait(v, "v")
                C.inc(v.tensor_tensor(outbuf[ti % 2][:, :], yrbuf[:, :],
                                      x1rbuf[:, :], ALU.add), "v",
                      mark=f"out_{X}_{ti}")

    def run_pass(pn):
        C.begin(pn, sems if pn == 2 else None)
        if pn == 1:
            emit_sync(_FakeEng())
            emit_gpsimd(_FakeEng())
            emit_scalar(_FakeEng())
            emit_vector(_FakeEng())
            emit_tensor(_FakeEng())
            return
        with nc.Block() as block:
            @block.sync
            def _(q):
                emit_sync(q)

            @block.gpsimd
            def _(g):
                emit_gpsimd(g)

            @block.scalar
            def _(s):
                emit_scalar(s)

            @block.vector
            def _(v):
                emit_vector(v)

            @block.tensor
            def _(t):
                emit_tensor(t)

    run_pass(1)
    cnt1 = dict(C.cnt)
    run_pass(2)
    assert C.cnt == cnt1, (C.cnt, cnt1)
    st.close()
    return nc


def _u8view(wq_int32):
    a = np.ascontiguousarray(np.asarray(wq_int32))
    assert a.dtype == np.int32
    return np.ascontiguousarray(a.view(np.uint8)[..., ::4])


def _stage_w_hd(wq):
    a = _u8view(wq).reshape(HB, 128, 16, 128)
    a = np.ascontiguousarray(a.transpose(0, 3, 2, 1))
    return a.reshape(HB, 128, 2048)


def _stage_w_dh(wq):
    a = np.ascontiguousarray(_u8view(wq).T)
    return a.reshape(HB, 128, 2048)


def kernel(L0_attn_u, L0_attn_Lambda_unconstrained_re, L0_attn_Lambda_im,
           L0_attn_B_re, L0_attn_B_im, L0_attn_C_re, L0_attn_C_im,
           L0_attn_D, L0_attn_log_Delta, L0_ffn_wq_gate, L0_ffn_s_gate,
           L0_ffn_wq_up, L0_ffn_s_up, L0_ffn_wq_down, L0_ffn_s_down):
    u = np.ascontiguousarray(np.asarray(L0_attn_u, dtype=np.float32))
    wg = _stage_w_hd(L0_ffn_wq_gate)
    wu = _stage_w_hd(L0_ffn_wq_up)
    wd = _stage_w_dh(L0_ffn_wq_down)
    sg = np.ascontiguousarray(
        np.asarray(L0_ffn_s_gate, np.float32).reshape(HB, 128).T)
    su = np.ascontiguousarray(
        np.asarray(L0_ffn_s_up, np.float32).reshape(HB, 128).T)
    sd = np.ascontiguousarray(np.asarray(L0_ffn_s_down,
                                         np.float32).reshape(1, D))
    bre = np.ascontiguousarray(np.asarray(L0_attn_B_re, np.float32))
    bim = np.ascontiguousarray(np.asarray(L0_attn_B_im, np.float32))
    cret = np.ascontiguousarray(np.asarray(L0_attn_C_re, np.float32).T)
    cimt = np.ascontiguousarray(np.asarray(L0_attn_C_im, np.float32).T)
    drow = np.ascontiguousarray(np.asarray(L0_attn_D,
                                           np.float32).reshape(1, D))
    lunc = np.ascontiguousarray(
        np.asarray(L0_attn_Lambda_unconstrained_re,
                   np.float32).reshape(NCH, 1))
    lim = np.ascontiguousarray(np.asarray(L0_attn_Lambda_im,
                                          np.float32).reshape(NCH, 1))
    logd = np.ascontiguousarray(
        np.asarray(L0_attn_log_Delta, np.float32)[:NCH].reshape(NCH, 1))
    zeros_u = np.zeros((T, D), np.float32)

    nc = build_program()
    in_maps = []
    for c in range(8):
        b, h = c // 2, c % 2
        mine = np.ascontiguousarray(u[b, h * T:(h + 1) * T, :])
        prev = np.ascontiguousarray(u[b, 0:T, :]) if h == 1 else zeros_u
        in_maps.append(dict(
            u=mine, uprev=prev, wg=wg, wu=wu, wd=wd, sg=sg, su=su, sd=sd,
            bre=bre, bim=bim, cret=cret, cimt=cimt, drow=drow, lunc=lunc,
            lim=lim, logd=logd))
    res = run_bass_kernel_spmd(nc, in_maps, core_ids=list(range(8)))
    out = np.empty((B, L, D), np.float32)
    for c in range(8):
        b, h = c // 2, c % 2
        out[b, h * T:(h + 1) * T, :] = np.asarray(res.results[c]["out"])
    return out


# version 04
# speedup vs baseline: 3.1777x; 1.0686x over previous
import contextlib
import numpy as np
import ml_dtypes
import concourse.bass as bass
import concourse.mybir as mybir
from concourse.bass_utils import run_bass_kernel_spmd

F32 = mybir.dt.float32
BF16 = mybir.dt.bfloat16
U8 = mybir.dt.uint8
I32 = mybir.dt.int32
AF = mybir.ActivationFunctionType
ALU = mybir.AluOpType
AX = mybir.AxisListType

B, L, D = 4, 4096, 2048
HID = 5632
NCH = 32
T = 2048
NT = 16
HB = 44
TH = 1024
PI = float(np.pi)
TWO_PI = float(2.0 * np.pi)
EPS_LN = 1e-5
KB = 1024


class Ctx:

    KEYS = ("d_par", "d_u", "d_x0", "d_x1w", "d_x1r", "d_xnw", "d_xnr",
            "d_w", "d_yw", "d_yr", "d_out", "v", "s", "t", "g")

    def __init__(self):
        self.marks = {}
        self.pass_num = 0
        self.cnt = {}
        self.sems = None

    def begin(self, pass_num, sems=None):
        self.pass_num = pass_num
        self.cnt = {k: 0 for k in self.KEYS}
        self.sems = sems

    def inc(self, inst, key, by=1, mark=None):
        if self.pass_num == 2:
            inst.then_inc(self.sems[key], by)
        self.cnt[key] += by
        if mark is not None:
            if self.pass_num == 1:
                assert mark not in self.marks, f"dup mark {mark}"
                self.marks[mark] = (key, self.cnt[key])
            else:
                assert self.marks[mark] == (key, self.cnt[key]), \
                    f"mark drift {mark}"
        return self.cnt[key]

    def wait(self, eng, mark, delta=0):
        if self.pass_num == 2:
            key, val = self.marks[mark]
            if val + delta > 0:
                eng.wait_ge(self.sems[key], val + delta)

    def wait_cnt(self, eng, key, val):
        if self.pass_num == 2 and val > 0:
            eng.wait_ge(self.sems[key], val)

    def selfwait(self, eng, key):
        if self.pass_num == 2 and self.cnt[key] > 0:
            eng.wait_ge(self.sems[key], self.cnt[key])


class _FakeInst:
    def then_inc(self, *a, **k):
        return self


class _FakeEng:
    def __getattr__(self, name):
        return lambda *a, **k: _FakeInst()


def build_program():
    nc = bass.Bass("TRN2", target_bir_lowering=False, debug=False)

    d_in = {}

    def din(name, shape, dt):
        d_in[name] = nc.dram_tensor(name, shape, dt, kind="ExternalInput").ap()

    din("u", [T, D], F32)
    din("uprev", [T, D], F32)
    din("wg", [HB, 128, D], U8)
    din("wu", [HB, 128, D], U8)
    din("wd", [HB, 128, D], U8)
    din("sg", [128, HB], F32)
    din("su", [128, HB], F32)
    din("sd", [1, D], F32)
    din("bre", [NCH, D], F32)
    din("bim", [NCH, D], F32)
    din("cret", [NCH, D], F32)
    din("cimt", [NCH, D], F32)
    din("drow", [1, D], F32)
    din("lunc", [NCH, 1], F32)
    din("lim", [NCH, 1], F32)
    din("logd", [NCH, 1], F32)
    out_hbm = nc.dram_tensor("out", [T, D], F32, kind="ExternalOutput").ap()
    x1_spill = nc.dram_tensor("x1sp", [T, D], F32, kind="Internal").ap()
    xnt_spill = nc.dram_tensor("xntsp", [128, 16 * TH], BF16,
                               kind="Internal").ap()
    y_spill = nc.dram_tensor("ysp", [TH, D], F32, kind="Internal").ap()

    st = contextlib.ExitStack()
    ARENA = 207 * KB
    arena = st.enter_context(nc.sbuf_tensor("arena", [128, ARENA], U8)).ap()

    def vf32(off_kb, n):
        off = int(off_kb * KB)
        assert off % 4 == 0 and off + 4 * n <= ARENA, (off_kb, n)
        return arena[:, off:off + 4 * n].bitcast(F32)

    def vbf(off_kb, n):
        off = int(off_kb * KB)
        assert off % 2 == 0 and off + 2 * n <= ARENA, (off_kb, n)
        return arena[:, off:off + 2 * n].bitcast(BF16)

    def vi32(off_kb, n):
        off = int(off_kb * KB)
        return arena[:, off:off + 4 * n].bitcast(I32)

    t1n_full = vbf(0, 32768)
    wd_lo = vbf(0, 11264)
    wd_hi = vbf(22, 11264)
    wgbuf = [vbf(44, 2048), vbf(48, 2048)]
    wubuf = [vbf(52, 2048), vbf(56, 2048)]
    us_s1 = vbf(60, 1024)
    yrbuf = vf32(0, 2048)
    x1rbuf = vf32(8, 2048)
    outbuf = [vf32(16, 2048), vf32(24, 2048)]

    xnt = vbf(64, 16384)
    drow_f = vf32(70, 2048)[0:1, :]
    sdrow = vf32(78, 2048)[0:1, :]
    drow_bf = vbf(86, 2048)[0:1, :]

    ubuf = [vf32(96, 2048), vf32(104, 2048)]
    t1nT = [vbf(112, 2048), vbf(116, 2048)]
    bu_re_m = vf32(120, 2048)[0:NCH, :]
    bu_im_m = vf32(128, 2048)[0:NCH, :]
    bu_re_p = vf32(136, 2048)[0:NCH, :]
    bu_im_p = vf32(144, 2048)[0:NCH, :]
    cosT = vf32(152, 2048)[0:NCH, :]
    sinT = vf32(160, 2048)[0:NCH, :]
    it_fwd = vf32(168, 2048)[0:NCH, :]
    rp_v = vf32(168, 2048)[0:NCH, :]
    c2re_v = vbf(176, 2048)[0:NCH, :]
    c2im_v = vbf(180, 2048)[0:NCH, :]
    bbar_sA = vf32(96, 2048)[0:NCH, :]
    bbar_sB = vf32(104, 2048)[0:NCH, :]
    bbar_re = vbf(112, 2048)[0:NCH, :]
    bbar_im = vbf(116, 2048)[0:NCH, :]
    ph_p = vf32(120, 2048)[0:NCH, :]
    ph_q = vf32(128, 2048)[0:NCH, :]
    ph_qi = vi32(136, 2048)[0:NCH, :]
    ph_r = vf32(144, 2048)[0:NCH, :]
    tA = vf32(96, 2048)[0:NCH, :]
    tB = vf32(104, 2048)[0:NCH, :]
    tC = vf32(112, 2048)[0:NCH, :]
    bt_re = vf32(136, 2048)[0:NCH, :]
    bt_im = vf32(144, 2048)[0:NCH, :]
    xs_re = vbf(120, 2048)[0:NCH, :]
    xs_im = vbf(124, 2048)[0:NCH, :]
    tmpD = vf32(96, 2048)
    ybuf = vf32(104, 2048)
    x1buf = [vf32(112, 2048), vf32(128, 2048)]
    x0buf = [vf32(136, 2048), vf32(144, 2048)]
    xnbuf = vbf(152, 2048)
    xnt_stage = vbf(156, 2048)
    Hbuf = vbf(96, 45056)

    d_rep_bf = vbf(184, 2048)
    sdrep = vf32(188, 2048)

    id_bf = vbf(196, 128)
    onesrow_bf = vbf(196.25, 128)[0:1, :]
    onesrow_f = vf32(196.5, 128)[0:1, :]
    sgbuf = vf32(197, HB)
    subuf = vf32(197.25, HB)
    statbuf = vf32(197.5, 24)
    mvbuf = vf32(197.75, 2)
    colbuf = vf32(198, 32)
    bret_bf = vbf(198.25, 512)
    bimt_bf = vbf(199.25, 512)
    ystage = [vf32(200.25, 512), vf32(202.25, 512)]
    ln4stats = vf32(204.25, 192)
    h2buf = vbf(205, 1024)

    pA = st.enter_context(nc.psum_tensor("pA", [128, 2048], F32)).ap()
    pB = st.enter_context(nc.psum_tensor("pB", [128, 1024], F32)).ap()
    pC = st.enter_context(nc.psum_tensor("pC", [128, 512], F32)).ap()
    pD = st.enter_context(nc.psum_tensor("pD", [128, 512], F32)).ap()
    pBbf = pB.bitcast(BF16)

    sems = {k: st.enter_context(nc.semaphore(k)) for k in Ctx.KEYS}
    C = Ctx()

    p_lunc = colbuf[0:NCH, 0:1]
    p_lim = colbuf[0:NCH, 1:2]
    p_logd = colbuf[0:NCH, 2:3]
    c_delta = colbuf[0:NCH, 3:4]
    c_lamre = colbuf[0:NCH, 4:5]
    c_lnr = colbuf[0:NCH, 5:6]
    c_theta = colbuf[0:NCH, 6:7]
    c_r = colbuf[0:NCH, 7:8]
    c_are = colbuf[0:NCH, 8:9]
    c_aim = colbuf[0:NCH, 9:10]
    c_t1 = colbuf[0:NCH, 10:11]
    c_t2 = colbuf[0:NCH, 11:12]
    c_t3 = colbuf[0:NCH, 12:13]
    c_t4 = colbuf[0:NCH, 13:14]
    c_dre = colbuf[0:NCH, 14:15]
    c_dim = colbuf[0:NCH, 15:16]
    c_eps = colbuf[:, 16:17]
    c_cre = colbuf[0:NCH, 17:18]
    c_cim = colbuf[0:NCH, 18:19]
    c_sr = colbuf[0:NCH, 19:20]
    c_si = colbuf[0:NCH, 20:21]
    c_negm = colbuf[:, 21:22]
    c_rstd = colbuf[:, 22:23]
    c_sd = colbuf[:, 23:24]
    c_Tlnr = colbuf[0:NCH, 24:25]
    c_i32 = colbuf.bitcast(I32)[0:NCH, 25:26]
    c_sinth = colbuf[0:NCH, 26:27]
    c_sh = colbuf[0:NCH, 27:28]
    c_costh = colbuf[0:NCH, 28:29]
    c_thred = colbuf[0:NCH, 29:30]
    p_bre = bu_re_m
    p_bim = bu_im_m
    p_cret = cosT
    p_cimt = sinT

    def Hcols(hb, tg):
        return Hbuf[:, hb * TH + tg * 512: hb * TH + (tg + 1) * 512]

    def emit_sync(q):
        C.inc(q.dma_start(p_lunc, d_in["lunc"]), "d_par", 16)
        C.inc(q.dma_start(p_lim, d_in["lim"]), "d_par", 16)
        C.inc(q.dma_start(p_logd, d_in["logd"]), "d_par", 16)
        C.inc(q.dma_start(p_bre, d_in["bre"]), "d_par", 16)
        C.inc(q.dma_start(p_bim, d_in["bim"]), "d_par", 16)
        C.inc(q.dma_start(p_cret, d_in["cret"]), "d_par", 16)
        C.inc(q.dma_start(p_cimt, d_in["cimt"]), "d_par", 16)
        C.inc(q.dma_start(drow_f, d_in["drow"]), "d_par", 16)
        C.inc(q.dma_start(sdrow, d_in["sd"]), "d_par", 16)
        C.inc(q.dma_start(sgbuf[:, :], d_in["sg"]), "d_par", 16)
        C.inc(q.dma_start(subuf[:, :], d_in["su"]), "d_par", 16,
              mark="d_params")

        for j in range(2 * NT):
            src = d_in["uprev"] if j < NT else d_in["u"]
            i = j % NT
            if j < 2:
                C.wait(q, "v_bbar")
            else:
                C.wait(q, f"uapply_{j-2}")
            C.inc(q.dma_start(ubuf[j % 2][:, :],
                              src[i * 128:(i + 1) * 128, :]),
                  "d_u", 16, mark=f"du_{j}")

        for i in range(NT):
            if i < 2:
                C.wait(q, "v_scan_done")
            else:
                C.wait(q, f"x1_{i-2}")
            C.inc(q.dma_start(x0buf[i % 2][:, :],
                              d_in["u"][i * 128:(i + 1) * 128, :]),
                  "d_x0", 16, mark=f"dx0_{i}")
            if i >= 1:
                C.wait(q, f"x1_{i-1}")
                C.inc(q.dma_start(x1_spill[(i - 1) * 128:i * 128, :],
                                  x1buf[(i - 1) % 2][:, :]), "d_x1w", 16,
                      mark=f"dx1w_{i-1}")
            if i >= 9:
                C.wait(q, f"xnev_{i-1}")
                C.inc(q.dma_start(
                    xnt_spill.rearrange("p (k ti tt) -> p k ti tt",
                                        k=16, ti=8)[:, :, i - 9, :],
                    xnt_stage.rearrange("p (k tt) -> p k tt", k=16)),
                    "d_xnw", 16, mark=f"dxnw_{i-9}")
        C.wait(q, f"x1_{NT-1}")
        C.inc(q.dma_start(x1_spill[(NT - 1) * 128:NT * 128, :],
                          x1buf[(NT - 1) % 2][:, :]), "d_x1w", 16,
              mark=f"dx1w_{NT-1}")
        C.wait(q, f"xnev_{NT-1}")
        C.inc(q.dma_start(
            xnt_spill.rearrange("p (k ti tt) -> p k ti tt", k=16, ti=8)[
                :, :, 7, :],
            xnt_stage.rearrange("p (k tt) -> p k tt", k=16)),
            "d_xnw", 16, mark="dxnw_7")

        for X in range(2):
            if X == 1:
                C.wait(q, "s1mm_0_43")
                C.inc(q.dma_start(xnt[:, :], xnt_spill[:, :]), "d_xnr", 16,
                      mark="dxnr")
            for dg in range(4):
                for ti in range(8):
                    par = (dg * 8 + ti) % 2
                    C.wait(q, f"ys_{X}_{dg}_{ti}")
                    C.inc(q.dma_start(
                        y_spill[ti * 128:(ti + 1) * 128,
                                dg * 512:(dg + 1) * 512],
                        ystage[par][:, :]), "d_yw", 16,
                        mark=f"dyw_{X}_{dg}_{ti}")
            C.wait(q, f"s2f_{X}_3_7")
            for ti in range(8):
                gt = X * 8 + ti
                if ti >= 1:
                    C.wait(q, f"ln4apply_{X}_{ti-1}")
                C.inc(q.dma_start(yrbuf[:, :],
                                  y_spill[ti * 128:(ti + 1) * 128, :]),
                      "d_yr", 16, mark=f"dyr_{X}_{ti}")
                if ti >= 1:
                    C.wait(q, f"out_{X}_{ti-1}")
                C.inc(q.dma_start(x1rbuf[:, :],
                                  x1_spill[gt * 128:(gt + 1) * 128, :]),
                      "d_x1r", 16, mark=f"dx1r_{X}_{ti}")
                C.wait(q, f"out_{X}_{ti}")
                C.inc(q.dma_start(out_hbm[gt * 128:(gt + 1) * 128, :],
                                  outbuf[ti % 2][:, :]), "d_out", 16,
                      mark=f"dout_{X}_{ti}")
        C.wait_cnt(q, "d_out", 16 * 16)

    def emit_gpsimd(g):
        C.inc(g.iota(it_fwd[:, :], pattern=[[1, T]], base=1,
                     channel_multiplier=0,
                     allow_small_or_imprecise_dtypes=True), "g", mark="g_itf")
        C.inc(g.memset(id_bf[:, :], 1.0), "g")
        C.selfwait(g, "g")
        C.inc(g.affine_select(id_bf[:, :], id_bf[:, :], pattern=[[-1, 128]],
                              compare_op=ALU.is_equal, fill=0.0, base=0,
                              channel_multiplier=1), "g", mark="g_id")
        C.inc(g.memset(onesrow_bf[:, :], 1.0), "g")
        C.inc(g.memset(onesrow_f[:, :], 1.0), "g")
        C.inc(g.memset(c_eps, EPS_LN), "g", mark="g_setup")

        for X in range(2):
            for hb in range(HB):
                par = hb % 2
                if X == 0 and hb < 2:
                    C.wait(g, "yev_15")
                else:
                    pX, phb = (X, hb - 2) if hb >= 2 else (0, HB - 2 + hb)
                    C.wait(g, f"s1mm_{pX}_{phb}")
                C.inc(g.dma_start(wgbuf[par][:, :], d_in["wg"][hb, :, :]),
                      "d_w", 16)
                C.inc(g.dma_start(wubuf[par][:, :], d_in["wu"][hb, :, :]),
                      "d_w", 16, mark=f"wconv_{X}_{hb}")
            for dg in range(4):
                if dg == 0:
                    if X == 0:
                        C.wait(g, "yev_15")
                    else:
                        C.wait_cnt(g, "d_out", 16 * 8)
                        C.wait(g, "s2lo_0_3")
                else:
                    C.wait(g, f"s2lo_{X}_{dg-1}")
                C.inc(g.dma_start(
                    wd_lo.rearrange("p (h dd) -> p h dd", h=22),
                    d_in["wd"][0:22, :, dg * 512:(dg + 1) * 512].rearrange(
                        "h p dd -> p h dd")), "d_w", 16,
                    mark=f"dwdlo_{X}_{dg}")
                if dg == 0:
                    if X == 1:
                        C.wait(g, "s2f_0_3_7")
                else:
                    C.wait(g, f"s2f_{X}_{dg-1}_7")
                C.inc(g.dma_start(
                    wd_hi.rearrange("p (h dd) -> p h dd", h=22),
                    d_in["wd"][22:44, :, dg * 512:(dg + 1) * 512].rearrange(
                        "h p dd -> p h dd")), "d_w", 16,
                    mark=f"dwdhi_{X}_{dg}")

    def emit_scalar(s):
        C.wait(s, "d_params")
        C.inc(s.activation(c_delta, p_logd, AF.Exp), "s")
        C.inc(s.activation(c_t1, p_lunc, AF.Abs), "s")
        C.selfwait(s, "s")
        C.inc(s.activation(c_t2, c_t1, AF.Exp, scale=-1.0), "s")
        C.selfwait(s, "s")
        C.inc(s.activation(c_t3, c_t2, AF.Ln, bias=1.0), "s", mark="s_p1")
        C.wait(s, "v_lnr")
        C.inc(s.activation(c_r, c_lnr, AF.Exp), "s", mark="s_r")
        C.wait(s, "v_thred")
        C.inc(s.activation(c_sinth, c_thred, AF.Sin), "s")
        C.inc(s.activation(c_sh, c_thred, AF.Sin, scale=0.5), "s",
              mark="s_sincos")
        C.wait(s, "v_pred")
        C.inc(s.activation(sinT[:, :], ph_p[:, :], AF.Sin), "s")
        C.inc(s.activation(ph_q[:, :], ph_p[:, :], AF.Abs), "s",
              mark="s_abs")
        C.wait(s, "v_carg")
        C.inc(s.activation(cosT[:, :], ph_r[:, :], AF.Sin), "s",
              mark="s_tables")
        C.inc(s.activation(drow_bf, drow_f, AF.Copy), "s", mark="s_dbf")

        for j in range(2 * NT):
            C.wait(s, f"aggr_{j}")
            C.inc(s.activation(c_sd, mvbuf[:, 1:2], AF.Sqrt, bias=c_eps),
                  "s", mark=f"sqrt_{j}")
        C.wait(s, "v_rpe")
        C.inc(s.activation(rp_v[:, :], tC[:, :], AF.Exp, scale=-1.0,
                           bias=c_Tlnr), "s", mark="s_rpow")
        for i in range(NT):
            C.wait(s, f"yaggr_{i}")
            C.inc(s.activation(c_sd, mvbuf[:, 1:2], AF.Sqrt, bias=c_eps),
                  "s", mark=f"ysqrt_{i}")
            C.wait(s, f"x1aggr_{i}")
            C.inc(s.activation(c_sd, mvbuf[:, 1:2], AF.Sqrt, bias=c_eps),
                  "s", mark=f"x1sqrt_{i}")
        for X in range(2):
            for hb in range(HB):
                C.wait(s, f"gmm_{X}_{hb}")
                if hb >= 1 or X == 1:
                    pX, phb = (X, hb - 1) if hb >= 1 else (0, HB - 1)
                    C.wait(s, f"h_{pX}_{phb}")
                gsrc = pA if hb % 2 == 0 else pB
                for tg in range(2):
                    C.inc(s.activation(h2buf[:, tg * 512:(tg + 1) * 512],
                                       gsrc[:, tg * 512:(tg + 1) * 512],
                                       AF.Silu, scale=sgbuf[:, hb:hb + 1]),
                          "s", mark=(f"gs_{X}_{hb}" if tg == 1 else None))
            for dg in range(4):
                C.wait(s, f"dwdlo_{X}_{dg}")
                C.inc(s.activation(wd_lo[:, :], wd_lo[:, :], AF.Copy,
                                   bias=-128.0), "s", mark=f"wdlo_{X}_{dg}")
                C.wait(s, f"dwdhi_{X}_{dg}")
                C.inc(s.activation(wd_hi[:, :], wd_hi[:, :], AF.Copy,
                                   bias=-128.0), "s", mark=f"wdhi_{X}_{dg}")
            for ti in range(8):
                C.wait(s, f"l4aggr_{X}_{ti}")
                C.inc(s.activation(c_sd, mvbuf[:, 1:2], AF.Sqrt, bias=c_eps),
                      "s", mark=f"l4sqrt_{X}_{ti}")

    def emit_tensor(t):
        C.wait(t, "g_id")
        C.wait(t, "v_bbar")
        for kt in range(16):
            C.inc(t.transpose(pBbf[:, kt * 32:(kt + 1) * 32],
                              bbar_re[:, kt * 128:(kt + 1) * 128],
                              id_bf[0:NCH, 0:NCH]), "t")
        for kt in range(16):
            C.inc(t.transpose(pBbf[:, 512 + kt * 32:512 + (kt + 1) * 32],
                              bbar_im[:, kt * 128:(kt + 1) * 128],
                              id_bf[0:NCH, 0:NCH]), "t",
                  mark=("t_btr" if kt == 15 else None))
        C.wait(t, "s_dbf")
        for dg in range(4):
            C.inc(t.matmul(pA[:, dg * 512:(dg + 1) * 512], onesrow_bf,
                           drow_bf[:, dg * 512:(dg + 1) * 512],
                           start=True, stop=True), "t",
                  mark=("t_drep" if dg == 3 else None))
        C.wait(t, "v_drep")
        for dg in range(4):
            C.inc(t.matmul(pA[:, dg * 512:(dg + 1) * 512], onesrow_f,
                           sdrow[:, dg * 512:(dg + 1) * 512],
                           start=True, stop=True), "t",
                  mark=("t_sdrep" if dg == 3 else None))

        for j in range(2 * NT):
            C.wait(t, f"uapply_{j}")
            if j >= 1:
                C.wait(t, f"t1nT_{j-1}")
            i = j % NT
            src = t1n_full[:, i * 2048:(i + 1) * 2048]
            for kt in range(16):
                C.inc(t.transpose(pBbf[:, kt * 128:(kt + 1) * 128],
                                  src[:, kt * 128:(kt + 1) * 128],
                                  id_bf[:, :]), "t",
                      mark=(f"tr_{j}" if kt == 15 else None))
            C.wait(t, f"t1nT_{j}")
            if j >= 1:
                C.wait(t, f"buev_{j-1}")
            for kt in range(16):
                C.inc(t.matmul(pC[0:NCH, 0:128],
                               bret_bf[:, kt * 32:(kt + 1) * 32],
                               t1nT[j % 2][:, kt * 128:(kt + 1) * 128],
                               start=(kt == 0), stop=(kt == 15)), "t")
            for kt in range(16):
                C.inc(t.matmul(pD[0:NCH, 0:128],
                               bimt_bf[:, kt * 32:(kt + 1) * 32],
                               t1nT[j % 2][:, kt * 128:(kt + 1) * 128],
                               start=(kt == 0), stop=(kt == 15)), "t",
                      mark=(f"bu_{j}" if kt == 15 else None))

        C.wait(t, "v_scan_done")
        C.wait(t, "v_sdrep")
        for i in range(NT):
            if i >= 1:
                C.wait(t, f"yev_{i-1}")
            for dg in range(4):
                C.inc(t.matmul(pA[:, dg * 512:(dg + 1) * 512],
                               xs_re[:, i * 128:(i + 1) * 128],
                               c2re_v[:, dg * 512:(dg + 1) * 512],
                               start=True, stop=False), "t")
                C.inc(t.matmul(pA[:, dg * 512:(dg + 1) * 512],
                               xs_im[:, i * 128:(i + 1) * 128],
                               c2im_v[:, dg * 512:(dg + 1) * 512],
                               start=False, stop=True), "t",
                      mark=(f"ymm_{i}" if dg == 3 else None))
            C.wait(t, f"xnapply_{i}")
            if i == 0:
                C.wait(t, f"buev_{2*NT-1}")
            else:
                C.wait(t, f"xnev_{i-1}")
            for kt in range(16):
                C.inc(t.transpose(pBbf[:, kt * 128:(kt + 1) * 128],
                                  xnbuf[:, kt * 128:(kt + 1) * 128],
                                  id_bf[:, :]), "t",
                      mark=(f"xntr_{i}" if kt == 15 else None))

        PS2 = [pA[:, 0:512], pA[:, 512:1024], pA[:, 1024:1536],
               pA[:, 1536:2048], pB[:, 0:512], pB[:, 512:1024], pC, pD]
        for X in range(2):
            for hb in range(HB):
                par = hb % 2
                C.wait(t, f"wconv_{X}_{hb}")
                if X == 0 and hb == 0:
                    C.wait(t, "yev_15")
                if X == 0 and hb == 1:
                    C.wait(t, "xnev_15")
                if X == 1 and hb == 0:
                    C.wait(t, "dxnr")
                    C.wait(t, "gs_0_42")
                    C.wait(t, "uev_0_42")
                    C.wait(t, "ys_0_3_3")
                if X == 1 and hb == 1:
                    C.wait(t, "gs_0_43")
                    C.wait(t, "uev_0_43")
                    C.wait(t, "ys_0_3_7")
                if hb >= 2:
                    C.wait(t, f"gs_{X}_{hb-2}")
                    C.wait(t, f"uev_{X}_{hb-2}")
                if par == 0:
                    gps = [pA[:, 0:512], pA[:, 512:1024]]
                    ups = [pA[:, 1024:1536], pA[:, 1536:2048]]
                else:
                    gps = [pB[:, 0:512], pB[:, 512:1024]]
                    ups = [pC[:, :], pD[:, :]]
                for tg in range(2):
                    for kt in range(16):
                        C.inc(t.matmul(
                            gps[tg],
                            wgbuf[par][:, kt * 128:(kt + 1) * 128],
                            xnt[:, kt * TH + tg * 512:
                                kt * TH + (tg + 1) * 512],
                            start=(kt == 0), stop=(kt == 15)), "t",
                            mark=(f"gmm_{X}_{hb}"
                                  if tg == 1 and kt == 15 else None))
                for tg in range(2):
                    for kt in range(16):
                        C.inc(t.matmul(
                            ups[tg],
                            wubuf[par][:, kt * 128:(kt + 1) * 128],
                            xnt[:, kt * TH + tg * 512:
                                kt * TH + (tg + 1) * 512],
                            start=(kt == 0), stop=(kt == 15)), "t",
                            mark=(f"s1mm_{X}_{hb}"
                                  if tg == 1 and kt == 15 else None))
            for dg in range(4):
                for hb in range(HB):
                    if hb == 0:
                        C.wait(t, f"wdlo_{X}_{dg}")
                    if hb == 22:
                        C.wait(t, f"wdhi_{X}_{dg}")
                    wsrc = wd_lo if hb < 22 else wd_hi
                    wof = hb if hb < 22 else hb - 22
                    for ti in range(8):
                        if hb == 0:
                            if dg == 0:
                                C.wait(t, f"gs_{X}_{HB-1}")
                                C.wait(t, f"uev_{X}_{HB-1}")
                            else:
                                C.wait(t, f"ys_{X}_{dg-1}_{ti}")
                        C.inc(t.matmul(
                            PS2[ti],
                            Hbuf[:, hb * TH + ti * 128:
                                 hb * TH + (ti + 1) * 128],
                            wsrc[:, wof * 512:(wof + 1) * 512],
                            start=(hb == 0), stop=(hb == HB - 1)), "t",
                            mark=(f"s2f_{X}_{dg}_{ti}" if hb == HB - 1
                                  else (f"s2lo_{X}_{dg}"
                                        if hb == 21 and ti == 7 else None)))

    def emit_vector(v):
        C.wait(v, "d_params")
        C.wait(v, "g_setup")
        C.inc(v.tensor_scalar(c2re_v[:, :], p_cret, 2.0, None, ALU.mult),
              "v")
        C.inc(v.tensor_scalar(c2im_v[:, :], p_cimt, -2.0, None, ALU.mult),
              "v", mark="v_c2")
        C.wait(v, "s_p1")
        C.inc(v.tensor_scalar(c_t1, p_lunc, 0.0, None, ALU.max), "v")
        C.selfwait(v, "v")
        C.inc(v.tensor_tensor(c_t1, c_t1, c_t3, ALU.add), "v")
        C.selfwait(v, "v")
        C.inc(v.tensor_scalar(c_lamre, c_t1, 0.0101, -1.0, ALU.add,
                              ALU.mult), "v")
        C.selfwait(v, "v")
        C.inc(v.tensor_tensor(c_lnr, c_lamre, c_delta, ALU.mult), "v",
              mark="v_lnr")
        C.selfwait(v, "v")
        C.inc(v.tensor_scalar(c_Tlnr, c_lnr, float(T), None, ALU.mult), "v")
        C.inc(v.tensor_tensor(c_theta, p_lim, c_delta, ALU.mult), "v")
        C.selfwait(v, "v")
        C.inc(v.tensor_scalar(c_t2, c_theta, 1.0 / TWO_PI, None, ALU.mult),
              "v")
        C.selfwait(v, "v")
        C.inc(v.tensor_copy(c_i32, c_t2), "v")
        C.selfwait(v, "v")
        C.inc(v.tensor_copy(c_t2, c_i32), "v")
        C.selfwait(v, "v")
        C.inc(v.scalar_tensor_tensor(c_thred, c_t2, -TWO_PI, c_theta,
                                     ALU.mult, ALU.add), "v", mark="v_thred")
        C.wait(v, "s_sincos")
        C.inc(v.tensor_tensor(c_t2, c_sh, c_sh, ALU.mult), "v")
        C.selfwait(v, "v")
        C.inc(v.tensor_scalar(c_costh, c_t2, -2.0, 1.0, ALU.mult, ALU.add),
              "v")
        C.wait(v, "s_r")
        C.selfwait(v, "v")
        C.inc(v.tensor_tensor(c_are, c_r, c_costh, ALU.mult), "v")
        C.inc(v.tensor_tensor(c_aim, c_r, c_sinth, ALU.mult), "v")
        C.selfwait(v, "v")
        C.inc(v.tensor_scalar(c_t1, c_are, -1.0, None, ALU.add), "v")
        C.inc(v.tensor_tensor(c_t2, c_lamre, c_lamre, ALU.mult), "v")
        C.inc(v.tensor_tensor(c_t3, p_lim, p_lim, ALU.mult), "v")
        C.selfwait(v, "v")
        C.inc(v.tensor_tensor(c_t2, c_t2, c_t3, ALU.add), "v")
        C.selfwait(v, "v")
        C.inc(v.reciprocal(c_t2, c_t2), "v")
        C.selfwait(v, "v")
        C.inc(v.tensor_tensor(c_t3, c_t1, c_lamre, ALU.mult), "v")
        C.inc(v.tensor_tensor(c_t4, c_aim, p_lim, ALU.mult), "v")
        C.selfwait(v, "v")
        C.inc(v.tensor_tensor(c_t3, c_t3, c_t4, ALU.add), "v")
        C.selfwait(v, "v")
        C.inc(v.tensor_tensor(c_dre, c_t3, c_t2, ALU.mult), "v")
        C.inc(v.tensor_tensor(c_t3, c_aim, c_lamre, ALU.mult), "v")
        C.inc(v.tensor_tensor(c_t4, c_t1, p_lim, ALU.mult), "v")
        C.selfwait(v, "v")
        C.inc(v.tensor_tensor(c_t3, c_t3, c_t4, ALU.subtract), "v")
        C.selfwait(v, "v")
        C.inc(v.tensor_tensor(c_dim, c_t3, c_t2, ALU.mult), "v")
        C.selfwait(v, "v")
        C.inc(v.tensor_scalar(bbar_sA[:, :], p_bre, c_dre, None, ALU.mult),
              "v")
        C.inc(v.tensor_scalar(bbar_sB[:, :], p_bim, c_dim, None, ALU.mult),
              "v")
        C.selfwait(v, "v")
        C.inc(v.tensor_tensor(bbar_re[:, :], bbar_sA[:, :], bbar_sB[:, :],
                              ALU.subtract), "v")
        C.inc(v.tensor_scalar(bbar_sA[:, :], p_bim, c_dre, None, ALU.mult),
              "v")
        C.inc(v.tensor_scalar(bbar_sB[:, :], p_bre, c_dim, None, ALU.mult),
              "v")
        C.selfwait(v, "v")
        C.inc(v.tensor_tensor(bbar_im[:, :], bbar_sA[:, :], bbar_sB[:, :],
                              ALU.add), "v", mark="v_bbar")
        C.inc(v.tensor_scalar(ph_p[:, :], it_fwd[:, :], c_theta, None,
                              ALU.mult), "v")
        C.selfwait(v, "v")
        C.inc(v.tensor_scalar(ph_q[:, :], ph_p[:, :], 1.0 / TWO_PI, None,
                              ALU.mult), "v")
        C.selfwait(v, "v")
        C.inc(v.tensor_copy(ph_qi, ph_q[:, :]), "v")
        C.selfwait(v, "v")
        C.inc(v.tensor_copy(ph_r[:, :], ph_qi), "v")
        C.selfwait(v, "v")
        C.inc(v.scalar_tensor_tensor(ph_p[:, :], ph_r[:, :], -TWO_PI,
                                     ph_p[:, :], ALU.mult, ALU.add), "v",
              mark="v_pred")
        C.wait(v, "s_abs")
        C.inc(v.tensor_scalar(ph_r[:, :], ph_q[:, :], -1.0, PI / 2.0,
                              ALU.mult, ALU.add), "v", mark="v_carg")
        C.wait(v, "t_btr")
        C.inc(v.tensor_copy(bret_bf[:, :], pBbf[:, 0:512]), "v")
        C.inc(v.tensor_copy(bimt_bf[:, :], pBbf[:, 512:1024]), "v",
              mark="v_btrev")
        C.wait(v, "t_drep")
        C.inc(v.tensor_copy(d_rep_bf[:, :], pA[:, :]), "v", mark="v_drep")
        C.wait(v, "t_sdrep")
        C.inc(v.tensor_copy(sdrep[:, :], pA[:, :]), "v", mark="v_sdrep")

        for j in range(2 * NT):
            i = j % NT
            C.wait(v, f"du_{j}")
            for gq in range(4):
                C.inc(v.bn_stats(statbuf[:, 6 * gq:6 * (gq + 1)],
                                 ubuf[j % 2][:, 512 * gq:512 * (gq + 1)]),
                      "v")
            C.selfwait(v, "v")
            C.inc(v.bn_aggr(mvbuf[:, :], statbuf[:, :]), "v",
                  mark=f"aggr_{j}")
            C.wait(v, f"sqrt_{j}")
            C.inc(v.reciprocal(c_rstd, c_sd), "v")
            C.inc(v.tensor_scalar(c_negm, mvbuf[:, 0:1], -1.0, None,
                                  ALU.mult), "v")
            C.selfwait(v, "v")
            C.inc(v.tensor_scalar(t1n_full[:, i * 2048:(i + 1) * 2048],
                                  ubuf[j % 2][:, :], c_negm, c_rstd,
                                  ALU.add, ALU.mult), "v",
                  mark=f"uapply_{j}")
            C.wait(v, f"tr_{j}")
            for gq in range(4):
                C.inc(v.tensor_copy(t1nT[j % 2][:, 512 * gq:512 * (gq + 1)],
                                    pBbf[:, 512 * gq:512 * (gq + 1)]), "v",
                      mark=(f"t1nT_{j}" if gq == 3 else None))
            C.wait(v, f"bu_{j}")
            bre_dst = (bu_re_p if j < NT else bu_re_m)[
                :, i * 128:(i + 1) * 128]
            bim_dst = (bu_im_p if j < NT else bu_im_m)[
                :, i * 128:(i + 1) * 128]
            C.inc(v.tensor_copy(bre_dst, pC[0:NCH, 0:128]), "v")
            C.inc(v.tensor_copy(bim_dst, pD[0:NCH, 0:128]), "v",
                  mark=f"buev_{j}")

        C.inc(v.tensor_scalar(tC[:, :], it_fwd[:, :], c_lnr, None, ALU.mult),
              "v", mark="v_rpe")
        C.wait(v, "s_rpow")
        C.inc(v.tensor_tensor(tA[:, :], cosT[:, :], rp_v[:, :], ALU.mult),
              "v")
        C.inc(v.tensor_tensor(tB[:, :], sinT[:, :], rp_v[:, :], ALU.mult),
              "v")
        C.selfwait(v, "v")
        C.inc(v.tensor_tensor(tC[:, :], tA[:, :], bu_re_p[:, :], ALU.mult),
              "v")
        C.selfwait(v, "v")
        C.inc(v.reduce_sum(c_t1, tC[:, :], axis=AX.X), "v")
        C.inc(v.tensor_tensor(tC[:, :], tB[:, :], bu_im_p[:, :], ALU.mult),
              "v")
        C.selfwait(v, "v")
        C.inc(v.reduce_sum(c_t2, tC[:, :], axis=AX.X), "v")
        C.inc(v.tensor_tensor(tC[:, :], tA[:, :], bu_im_p[:, :], ALU.mult),
              "v")
        C.selfwait(v, "v")
        C.inc(v.reduce_sum(c_t3, tC[:, :], axis=AX.X), "v")
        C.inc(v.tensor_tensor(tC[:, :], tB[:, :], bu_re_p[:, :], ALU.mult),
              "v")
        C.selfwait(v, "v")
        C.inc(v.reduce_sum(c_t4, tC[:, :], axis=AX.X), "v")
        C.selfwait(v, "v")
        C.inc(v.tensor_tensor(c_sr, c_t1, c_t2, ALU.add), "v")
        C.inc(v.tensor_tensor(c_si, c_t3, c_t4, ALU.subtract), "v")
        C.selfwait(v, "v")
        C.inc(v.tensor_tensor(c_t1, cosT[:, T - 1:T], c_sr, ALU.mult), "v")
        C.inc(v.tensor_tensor(c_t2, sinT[:, T - 1:T], c_si, ALU.mult), "v")
        C.inc(v.tensor_tensor(c_t3, sinT[:, T - 1:T], c_sr, ALU.mult), "v")
        C.inc(v.tensor_tensor(c_t4, cosT[:, T - 1:T], c_si, ALU.mult), "v")
        C.selfwait(v, "v")
        C.inc(v.tensor_tensor(c_cre, c_t1, c_t2, ALU.subtract), "v")
        C.inc(v.tensor_tensor(c_cim, c_t3, c_t4, ALU.add), "v")
        C.selfwait(v, "v")
        C.inc(v.tensor_tensor(tA[:, :], cosT[:, :], bu_re_m[:, :], ALU.mult),
              "v")
        C.inc(v.tensor_tensor(tB[:, :], sinT[:, :], bu_im_m[:, :], ALU.mult),
              "v")
        C.selfwait(v, "v")
        C.inc(v.tensor_tensor(bt_re[:, :], tA[:, :], tB[:, :], ALU.add), "v")
        C.inc(v.tensor_tensor(tA[:, :], cosT[:, :], bu_im_m[:, :], ALU.mult),
              "v")
        C.inc(v.tensor_tensor(tB[:, :], sinT[:, :], bu_re_m[:, :], ALU.mult),
              "v")
        C.selfwait(v, "v")
        C.inc(v.tensor_tensor(bt_im[:, :], tA[:, :], tB[:, :], ALU.subtract),
              "v")
        C.inc(v.memset(tC[:, :], 1.0), "v")
        C.selfwait(v, "v")
        C.inc(v.tensor_scalar(tC[:, :], tC[:, :], c_r, None, ALU.mult), "v")
        C.selfwait(v, "v")
        C.inc(v.tensor_tensor_scan(tA[:, :], tC[:, :], bt_re[:, :], c_cre,
                                   ALU.mult, ALU.add), "v")
        C.inc(v.tensor_tensor_scan(tB[:, :], tC[:, :], bt_im[:, :], c_cim,
                                   ALU.mult, ALU.add), "v")
        C.selfwait(v, "v")
        C.inc(v.tensor_tensor(tC[:, :], cosT[:, :], tA[:, :], ALU.mult), "v")
        C.inc(v.tensor_tensor(bt_re[:, :], sinT[:, :], tB[:, :], ALU.mult),
              "v")
        C.selfwait(v, "v")
        C.inc(v.tensor_tensor(xs_re[:, :], tC[:, :], bt_re[:, :],
                              ALU.subtract), "v")
        C.inc(v.tensor_tensor(tC[:, :], cosT[:, :], tB[:, :], ALU.mult), "v")
        C.inc(v.tensor_tensor(bt_im[:, :], sinT[:, :], tA[:, :], ALU.mult),
              "v")
        C.selfwait(v, "v")
        C.inc(v.tensor_tensor(xs_im[:, :], tC[:, :], bt_im[:, :], ALU.add),
              "v", mark="v_scan_done")

        for i in range(NT):
            C.inc(v.tensor_tensor(tmpD[:, :], d_rep_bf[:, :],
                                  t1n_full[:, i * 2048:(i + 1) * 2048],
                                  ALU.mult), "v")
            C.wait(v, f"ymm_{i}")
            C.selfwait(v, "v")
            for gq in range(4):
                C.inc(v.tensor_tensor(ybuf[:, 512 * gq:512 * (gq + 1)],
                                      pA[:, 512 * gq:512 * (gq + 1)],
                                      tmpD[:, 512 * gq:512 * (gq + 1)],
                                      ALU.add), "v",
                      mark=(f"yev_{i}" if gq == 3 else None))
            C.selfwait(v, "v")
            for gq in range(4):
                C.inc(v.bn_stats(statbuf[:, 6 * gq:6 * (gq + 1)],
                                 ybuf[:, 512 * gq:512 * (gq + 1)]), "v")
            C.selfwait(v, "v")
            C.inc(v.bn_aggr(mvbuf[:, :], statbuf[:, :]), "v",
                  mark=f"yaggr_{i}")
            C.wait(v, f"ysqrt_{i}")
            C.inc(v.reciprocal(c_rstd, c_sd), "v")
            C.inc(v.tensor_scalar(c_negm, mvbuf[:, 0:1], -1.0, None,
                                  ALU.mult), "v")
            C.selfwait(v, "v")
            C.inc(v.tensor_scalar(tmpD[:, :], ybuf[:, :], c_negm, c_rstd,
                                  ALU.add, ALU.mult), "v")
            C.wait(v, f"dx0_{i}")
            if i >= 2:
                C.wait(v, f"dx1w_{i-2}")
            C.selfwait(v, "v")
            C.inc(v.tensor_tensor(x1buf[i % 2][:, :], tmpD[:, :],
                                  x0buf[i % 2][:, :], ALU.add), "v",
                  mark=f"x1_{i}")
            C.selfwait(v, "v")
            for gq in range(4):
                C.inc(v.bn_stats(statbuf[:, 6 * gq:6 * (gq + 1)],
                                 x1buf[i % 2][:, 512 * gq:512 * (gq + 1)]),
                      "v")
            C.selfwait(v, "v")
            C.inc(v.bn_aggr(mvbuf[:, :], statbuf[:, :]), "v",
                  mark=f"x1aggr_{i}")
            C.wait(v, f"x1sqrt_{i}")
            C.inc(v.reciprocal(c_rstd, c_sd), "v")
            C.inc(v.tensor_scalar(c_negm, mvbuf[:, 0:1], -1.0, None,
                                  ALU.mult), "v")
            C.selfwait(v, "v")
            C.inc(v.tensor_scalar(xnbuf[:, :], x1buf[i % 2][:, :], c_negm,
                                  c_rstd, ALU.add, ALU.mult), "v",
                  mark=f"xnapply_{i}")
            C.wait(v, f"xntr_{i}")
            if i < 8:
                for gq in range(4):
                    dst = xnt.rearrange("p (k tt) -> p k tt", k=16)[
                        :, 4 * gq:4 * (gq + 1), i * 128:(i + 1) * 128]
                    C.inc(v.tensor_copy(
                        dst,
                        pBbf[:, 512 * gq:512 * (gq + 1)].rearrange(
                            "p (k tt) -> p k tt", k=4)), "v",
                        mark=(f"xnev_{i}" if gq == 3 else None))
            else:
                if i >= 10:
                    C.wait(v, f"dxnw_{i-10}")
                for gq in range(4):
                    C.inc(v.tensor_copy(
                        xnt_stage[:, 512 * gq:512 * (gq + 1)],
                        pBbf[:, 512 * gq:512 * (gq + 1)]), "v",
                        mark=(f"xnev_{i}" if gq == 3 else None))

        for X in range(2):
            for hb in range(HB):
                if X == 0 and hb == 0:
                    C.wait(v, "dxnw_7")
                    C.wait(v, f"dx1w_{NT-1}")
                C.wait(v, f"s1mm_{X}_{hb}")
                if hb % 2 == 0:
                    ups = [pA[:, 1024:1536], pA[:, 1536:2048]]
                else:
                    ups = [pC[:, :], pD[:, :]]
                for tg in range(2):
                    C.inc(v.tensor_scalar(
                        us_s1[:, tg * 512:(tg + 1) * 512], ups[tg],
                        subuf[:, hb:hb + 1], None, ALU.mult), "v",
                        mark=(f"uev_{X}_{hb}" if tg == 1 else None))
                C.wait(v, f"gs_{X}_{hb}")
                C.selfwait(v, "v")
                for tg in range(2):
                    C.inc(v.tensor_tensor(
                        Hcols(hb, tg), h2buf[:, tg * 512:(tg + 1) * 512],
                        us_s1[:, tg * 512:(tg + 1) * 512], ALU.mult), "v",
                        mark=(f"h_{X}_{hb}" if tg == 1 else None))
            PS2v = [pA[:, 0:512], pA[:, 512:1024], pA[:, 1024:1536],
                    pA[:, 1536:2048], pB[:, 0:512], pB[:, 512:1024],
                    pC[:, :], pD[:, :]]
            for dg in range(4):
                for ti in range(8):
                    par = (dg * 8 + ti) % 2
                    C.wait(v, f"s2f_{X}_{dg}_{ti}")
                    k = dg * 8 + ti
                    if k >= 2:
                        pk = k - 2
                        C.wait(v, f"dyw_{X}_{pk // 8}_{pk % 8}")
                    elif X == 1:
                        C.wait(v, f"dyw_0_3_{6 + ti}")
                    C.inc(v.tensor_tensor(ystage[par][:, :], PS2v[ti],
                                          sdrep[:, dg * 512:(dg + 1) * 512],
                                          ALU.mult), "v")
                    C.selfwait(v, "v")
                    C.inc(v.bn_stats(
                        ln4stats[:, ti * 24 + dg * 6:ti * 24 + dg * 6 + 6],
                        ystage[par][:, :]), "v", mark=f"ys_{X}_{dg}_{ti}")
            for ti in range(8):
                C.wait(v, f"dyr_{X}_{ti}")
                C.inc(v.bn_aggr(mvbuf[:, :],
                                ln4stats[:, ti * 24:(ti + 1) * 24]), "v",
                      mark=f"l4aggr_{X}_{ti}")
                C.wait(v, f"l4sqrt_{X}_{ti}")
                C.inc(v.reciprocal(c_rstd, c_sd), "v")
                C.inc(v.tensor_scalar(c_negm, mvbuf[:, 0:1], -1.0, None,
                                      ALU.mult), "v")
                C.selfwait(v, "v")
                C.inc(v.tensor_scalar(yrbuf[:, :], yrbuf[:, :], c_negm,
                                      c_rstd, ALU.add, ALU.mult), "v",
                      mark=f"ln4apply_{X}_{ti}")
                C.wait(v, f"dx1r_{X}_{ti}")
                gk = X * 8 + ti
                if gk >= 2:
                    pgk = gk - 2
                    C.wait(v, f"dout_{pgk // 8}_{pgk % 8}")
                C.selfwait(v, "v")
                C.inc(v.tensor_tensor(outbuf[ti % 2][:, :], yrbuf[:, :],
                                      x1rbuf[:, :], ALU.add), "v",
                      mark=f"out_{X}_{ti}")

    def run_pass(pn):
        C.begin(pn, sems if pn == 2 else None)
        if pn == 1:
            emit_sync(_FakeEng())
            emit_gpsimd(_FakeEng())
            emit_scalar(_FakeEng())
            emit_vector(_FakeEng())
            emit_tensor(_FakeEng())
            return
        with nc.Block() as block:
            @block.sync
            def _(q):
                emit_sync(q)

            @block.gpsimd
            def _(g):
                emit_gpsimd(g)

            @block.scalar
            def _(s):
                emit_scalar(s)

            @block.vector
            def _(v):
                emit_vector(v)

            @block.tensor
            def _(t):
                emit_tensor(t)

    run_pass(1)
    cnt1 = dict(C.cnt)
    run_pass(2)
    assert C.cnt == cnt1, (C.cnt, cnt1)
    st.close()
    return nc


def _u8view(wq_int32):
    a = np.ascontiguousarray(np.asarray(wq_int32))
    assert a.dtype == np.int32
    return np.ascontiguousarray(a.view(np.uint8)[..., ::4])


def _stage_w_hd(wq):
    a = _u8view(wq).reshape(HB, 128, 16, 128)
    a = np.ascontiguousarray(a.transpose(0, 3, 2, 1))
    return a.reshape(HB, 128, 2048)


def _stage_w_dh(wq):
    a = np.ascontiguousarray(_u8view(wq).T)
    return a.reshape(HB, 128, 2048)


def kernel(L0_attn_u, L0_attn_Lambda_unconstrained_re, L0_attn_Lambda_im,
           L0_attn_B_re, L0_attn_B_im, L0_attn_C_re, L0_attn_C_im,
           L0_attn_D, L0_attn_log_Delta, L0_ffn_wq_gate, L0_ffn_s_gate,
           L0_ffn_wq_up, L0_ffn_s_up, L0_ffn_wq_down, L0_ffn_s_down):
    u = np.ascontiguousarray(np.asarray(L0_attn_u, dtype=np.float32))
    wg = _stage_w_hd(L0_ffn_wq_gate)
    wu = _stage_w_hd(L0_ffn_wq_up)
    wd = _stage_w_dh(L0_ffn_wq_down)
    sg = np.ascontiguousarray(
        np.asarray(L0_ffn_s_gate, np.float32).reshape(HB, 128).T)
    su = np.ascontiguousarray(
        np.asarray(L0_ffn_s_up, np.float32).reshape(HB, 128).T)
    sd = np.ascontiguousarray(np.asarray(L0_ffn_s_down,
                                         np.float32).reshape(1, D))
    bre = np.ascontiguousarray(np.asarray(L0_attn_B_re, np.float32))
    bim = np.ascontiguousarray(np.asarray(L0_attn_B_im, np.float32))
    cret = np.ascontiguousarray(np.asarray(L0_attn_C_re, np.float32).T)
    cimt = np.ascontiguousarray(np.asarray(L0_attn_C_im, np.float32).T)
    drow = np.ascontiguousarray(np.asarray(L0_attn_D,
                                           np.float32).reshape(1, D))
    lunc = np.ascontiguousarray(
        np.asarray(L0_attn_Lambda_unconstrained_re,
                   np.float32).reshape(NCH, 1))
    lim = np.ascontiguousarray(np.asarray(L0_attn_Lambda_im,
                                          np.float32).reshape(NCH, 1))
    logd = np.ascontiguousarray(
        np.asarray(L0_attn_log_Delta, np.float32)[:NCH].reshape(NCH, 1))
    zeros_u = np.zeros((T, D), np.float32)

    nc = build_program()
    in_maps = []
    for c in range(8):
        b, h = c // 2, c % 2
        mine = np.ascontiguousarray(u[b, h * T:(h + 1) * T, :])
        prev = np.ascontiguousarray(u[b, 0:T, :]) if h == 1 else zeros_u
        in_maps.append(dict(
            u=mine, uprev=prev, wg=wg, wu=wu, wd=wd, sg=sg, su=su, sd=sd,
            bre=bre, bim=bim, cret=cret, cimt=cimt, drow=drow, lunc=lunc,
            lim=lim, logd=logd))
    res = run_bass_kernel_spmd(nc, in_maps, core_ids=list(range(8)))
    out = np.empty((B, L, D), np.float32)
    for c in range(8):
        b, h = c // 2, c % 2
        out[b, h * T:(h + 1) * T, :] = np.asarray(res.results[c]["out"])
    return out
